# version 12
# baseline (speedup 1.0000x reference)
import numpy as np
import concourse.bass as bass
import concourse.mybir as mybir
from concourse.bass_utils import run_bass_kernel_spmd

F32 = mybir.dt.float32
BF16 = mybir.dt.bfloat16
I32 = mybir.dt.int32
AF = mybir.ActivationFunctionType
OP = mybir.AluOpType

TC = 4096
CH = 512
DC = 128
D = 1024
KC = D // 128
NCH = TC // CH
NDC = TC // DC
NEG = -30000.0
RG = [[0, 1], [2, 3], [4, 5], [6, 7]]
GAM = [float(1.0 - 2.0 ** (-5.0 - h)) for h in range(4)]
STAGES = ("l0a", "l0f", "l1a", "l1f")
RELAX = ()

ENGINES = ("tensor", "vector", "scalar", "gpsimd", "sync")


class Prog:
    def __init__(self, nc, sems):
        self.nc = nc
        self.sems = sems
        self.count = {k: 0 for k in sems}
        self.alias = {}
        self.qidx = {}
        self.last_item = {}
        self.g_read = {}
        self.g_write = {}
        self.runends = {}
        self.softhist = {}
        self.stream = {e: [] for e in ENGINES}
        self.water = {e: {k: 0 for k in sems} for e in ENGINES}

    def emit(self, eng, fn, *args, deps=(), inc=None, amt=None, **kw):
        is_dma = fn == "dma_start"
        rotated = False
        if inc is None:
            if is_dma:
                qk = "q_" + eng
                has_deps = any(d is not None and d[1] > 0 for d in deps)
                if self.last_item.get(eng) != qk or has_deps:
                    old_sem = f"{qk}{self.qidx[qk]}" if qk in self.qidx else None
                    if old_sem is not None:
                        self.runends.setdefault(old_sem, []).append(self.count[old_sem])
                    self.qidx[qk] = (self.qidx.get(qk, -1) + 1) % 16
                    rotated = True
                inc = f"{qk}{self.qidx[qk]}"
            else:
                inc = eng
        self.last_item[eng] = ("q_" + eng) if is_dma else None
        if amt is None:
            amt = 16 if is_dma else 1
        def tname(x):
            t = getattr(x, "tensor", None)
            if t is not None:
                n = getattr(t, "name", None)
            else:
                n = getattr(x, "name", None) if "TensorHandle" in type(x).__name__ else None
            if n is None or n.startswith("ps") or n == "x_res":
                return None
            return n
        wr_keys = ("out", "out_ap", "accum_out")
        reads, writes = [], []
        if fn == "memset":
            n = tname(args[0]) if args else None
            if n:
                writes.append(n)
        else:
            for k, v in kw.items():
                n = tname(v)
                if n:
                    (writes if k in wr_keys else reads).append(n)
            for a in args:
                n = tname(a)
                if n:
                    reads.append(n)
        deps = list(deps)
        if is_dma and rotated and self.count[inc] > 0:
            deps.append((inc, self.count[inc]))
        for n in writes:
            deps += self.g_read.get(n, [])
            for d in self.g_write.get(n, []):
                if not (is_dma and d[0].startswith("q_")):
                    deps.append(d)
        for n in reads:
            deps += self.g_write.get(n, [])
        real = []
        if (not is_dma) and eng in ("gpsimd", "vector", "scalar"):
            deps = deps + [(eng, self.count[eng])]
        for d in deps:
            if d is None:
                continue
            s, v = d
            if v > 0 and self.water[eng][s] < v:
                real.append((s, v))
                self.water[eng][s] = v
        self.count[inc] += amt
        tick = (inc, self.count[inc])
        self.stream[eng].append((real, fn, args, kw, tick, amt))
        for n in writes:
            self.g_write[n] = [tick]
            self.g_read[n] = []
        for n in reads:
            self.g_read.setdefault(n, []).append(tick)
        return tick

    def soft_barrier(self, key, lag=2):
        if key not in RELAX:
            self.barrier()
            return
        hist = self.softhist.setdefault(key, [])
        snap = [(k, v) for k, v in self.count.items() if v > 0]
        hist.append(snap)
        if len(hist) <= lag:
            return
        tgt = hist[-1 - lag]
        for e in ("tensor", "vector", "scalar", "gpsimd", "sync"):
            deps = []
            for k, v in tgt:
                kk = "~" + k if k.startswith("q_") else k
                deps.append((kk, v))
            self.count[e] += 1
            real = []
            for kk, v in deps:
                base = kk[1:] if kk.startswith("~") else kk
                if self.water[e][base] < v:
                    real.append((kk, v))
                    if not kk.startswith("~"):
                        self.water[e][kk] = v
            self.stream[e].append((real, "__nop__", (), {}, (e, self.count[e]), 1))

    def barrier(self):
        for qk, idx in self.qidx.items():
            cur = f"{qk}{idx}"
            if self.count.get(cur, 0) > 0:
                self.runends.setdefault(cur, []).append(self.count[cur])
        self.last_item = {}
        snap = [(k, v) for k, v in self.count.items() if v > 0]
        for e in ENGINES:
            self.count[e] += 1
            self.stream[e].append((
                [(k, v) for (k, v) in snap if self.water[e][k] < v],
                "__nop__", (), {}, (e, self.count[e]), 1))
            for k, v in snap:
                if self.water[e][k] < v:
                    self.water[e][k] = v

    def finalize(self, blk):
        milestones = {}
        for k, v in self.runends.items():
            milestones[k] = list(v)
        for k in self.count:
            if k.startswith("q_") and self.count[k] > 0:
                milestones.setdefault(k, []).append(self.count[k])
        for k in milestones:
            milestones[k] = sorted(set(milestones[k]))

        import bisect

        def roundup(sem, v):
            ms = milestones.get(sem)
            if not ms:
                return v
            i = bisect.bisect_left(ms, v)
            return ms[i] if i < len(ms) else ms[-1]

        def rounddown(sem, v):
            ms = milestones.get(sem)
            if not ms:
                return 0
            i = bisect.bisect_right(ms, v)
            return ms[i - 1] if i > 0 else 0

        for eng in ENGINES:
            for item in self.stream[eng]:
                deps = item[0]
                for i, (sm, v) in enumerate(deps):
                    if sm.startswith("~"):
                        base = sm[1:]
                        deps[i] = (base, rounddown(base, v))
                    elif sm.startswith("q_"):
                        deps[i] = (sm, roundup(sm, v))
        for eng in ENGINES:
            ins = self.stream[eng]
            if not ins:
                continue

            def fix(a):
                return a[:] if "TensorHandle" in type(a).__name__ else a

            def body(h, ins=ins):
                for (deps, fn, args, kw, tick, amt) in ins:
                    for (s, v) in deps:
                        h.wait_ge(self.sems[s], v)
                    if fn == "__nop__":
                        h.nop().then_inc(self.sems[tick[0]], amt)
                    else:
                        args = [fix(a) for a in args]
                        kw = {k: fix(v) for k, v in kw.items()}
                        getattr(h, fn)(*args, **kw).then_inc(self.sems[tick[0]], amt)
            getattr(blk, eng)(body)


class Ctx:
    def __init__(self, p, ap, sb, PS, nc):
        self.p, self.ap, self.sb, self.PS = p, ap, sb, PS
        self.nc = nc
        self.ps_owner = [("tensor", 0)] * 8
        self._phase_id = 0
        self._tog = 0

    def phase(self):
        import contextlib
        self._phase_id += 1
        return contextlib.ExitStack()

    def alloc(self, es, name, shape, dt):
        nm = f"ph{self._phase_id}_{name}"

        def rec(nm, shp):
            if len(shp) == 2:
                return es.enter_context(self.nc.sbuf_tensor(nm, list(shp), dt))
            return [rec(f"{nm}_{i}", shp[1:]) for i in range(shp[0])]
        t = rec(nm, list(shape))
        self.sb[name] = t
        return t

    @staticmethod
    def _dt(x):
        t = getattr(x, "tensor", None)
        return t.dtype if t is not None else x.dtype

    def load(self, dst, src, deps=(), q="sync"):
        if self._dt(dst) != self._dt(src):
            q = "gpsimd"
        return self.p.emit(q, "dma_start", deps=deps, out=dst, in_=src,
                           allow_slow_non_contiguous=True)

    def store(self, dst, src, deps=(), q="gpsimd"):
        if self._dt(dst) != self._dt(src):
            q = "gpsimd"
        return self.p.emit(q, "dma_start", deps=deps, out=dst, in_=src,
                           allow_slow_non_contiguous=True)

    def mm(self, bank, lhsT, rhs, start, stop, deps=(), rows=128, roff=0, cols=512, coff=0):
        v = self.PS[bank][roff:roff + rows, coff:coff + cols]
        d = list(deps)
        if start:
            d.append(self.ps_owner[bank])
        return self.p.emit("tensor", "matmul", deps=d, out=v, lhsT=lhsT, rhs=rhs,
                           start=start, stop=stop)

    def consume(self, bank, tick):
        self.ps_owner[bank] = tick

    @staticmethod
    def _haspsum(*ops):
        for o in ops:
            sp = getattr(o, "space", None)
            if sp is not None and "PSUM" in str(sp):
                return True
        return False

    def _psafe(self, eng, *ops):
        if eng != "gpsimd":
            return eng
        if self._haspsum(*ops):
            return "vector"
        return eng

    def _bal(self, eng, *ops):
        if eng != "vector" or self._haspsum(*ops):
            return eng
        self._tog = (self._tog + 1) % 2
        return "gpsimd" if self._tog == 0 else "vector"

    _POOL_OPS = (OP.mult, OP.add, OP.subtract)

    def tt(self, out, a, b, op, deps=(), eng="vector"):
        if op in self._POOL_OPS:
            eng = self._bal(eng, out, a, b)
        eng = self._psafe(eng, out, a, b)
        return self.p.emit(eng, "tensor_tensor", deps=deps, out=out, in0=a, in1=b, op=op)

    def ts(self, out, a, s1, s2, op0, op1=None, deps=(), eng="vector"):
        if isinstance(s1, (int, float)) and op0 in self._POOL_OPS and \
                (op1 is None or op1 in self._POOL_OPS):
            eng = self._bal(eng, out, a)
        eng = self._psafe(eng, out, a)
        if eng == "gpsimd" and not isinstance(s1, (int, float)):
            eng = "vector"
        if op1 is None:
            return self.p.emit(eng, "tensor_scalar", deps=deps, out=out, in0=a,
                               scalar1=s1, scalar2=None, op0=op0)
        return self.p.emit(eng, "tensor_scalar", deps=deps, out=out, in0=a,
                           scalar1=s1, scalar2=s2, op0=op0, op1=op1)

    def stt(self, out, in0, scalar, in1, op0, op1, deps=(), eng="vector"):
        eng = self._psafe(eng, out, in0, in1)
        if eng == "gpsimd" and not isinstance(scalar, (int, float)):
            eng = "vector"
        return self.p.emit(eng, "scalar_tensor_tensor", deps=deps, out=out, in0=in0,
                           scalar=scalar, in1=in1, op0=op0, op1=op1)

    def cp(self, out, a, deps=(), eng="vector"):
        if self._haspsum(a) and not self._haspsum(out):
            return self.p.emit("scalar", "copy", deps=deps, out=out, in_=a)
        eng = self._psafe(eng, out, a)
        return self.p.emit(eng, "tensor_copy", deps=deps, out=out, in_=a)

    def act(self, out, a, fn, bias=0.0, scale=1.0, accum=None, deps=()):
        kw = dict(out=out, in_=a, func=fn, bias=bias, scale=scale)
        if accum is not None:
            kw["accum_out"] = accum
        return self.p.emit("scalar", "activation", deps=deps, **kw)

    def scan(self, out, d0, init, op0, deps=(), eng="vector"):
        return self.p.emit(eng, "tensor_tensor_scan", deps=deps, out=out, data0=d0,
                           data1=d0, initial=float(init), op0=op0, op1=OP.bypass)

    def recip(self, out, a, deps=(), eng="vector"):
        return self.p.emit(eng, "reciprocal", deps=deps, out=out, in_=a)

    def redsum(self, out, a, deps=(), eng="vector"):
        return self.p.emit(eng, "reduce_sum", deps=deps, out=out, in_=a,
                           axis=mybir.AxisListType.X)

    def gather(self, snd, rcv, deps=()):
        return self.p.emit("gpsimd", "collective_compute", "AllGather", OP.bypass,
                           deps=deps, inc="cc", amt=1,
                           replica_groups=RG, ins=[snd.opt()], outs=[rcv.opt()])

    def bcast(self, bank, row, deps, cols=CH, coff=0):
        return self.mm(bank, self.sb["ones_row_f"][0:1, :], row, True, True,
                       deps=deps, cols=cols, coff=coff)

    def rowT(self, bank, row, deps, n=128, coff=0):
        return self.mm(bank, row, self.sb["one_f"][0:1, 0:1], True, True,
                       deps=deps, rows=n, cols=1, coff=coff)


def host_tables(shalf):
    t = {}
    HK = 256
    inv_freq = 1.0 / (10000.0 ** (np.arange(0, HK, 2, dtype=np.float64) / HK))
    pos = np.arange(shalf * TC, (shalf + 1) * TC, dtype=np.float64)
    fr = inv_freq[:, None] * pos[None, :]
    t["rope_cos"] = np.cos(fr).astype(np.float32)
    t["rope_sin"] = np.sin(fr).astype(np.float32)
    gam = 1.0 - np.exp2(-5.0 - np.arange(4, dtype=np.float64))
    i = np.arange(CH, dtype=np.float64)
    sc = 1.0 / 16.0
    mask = np.where(i[None, :] - i[:, None] >= 0.0,
                    gam[:, None, None] ** (i[None, :] - i[:, None]), 0.0) * sc
    t["ret_maskT"] = mask.astype(np.float32)
    t["ret_cross"] = np.broadcast_to((gam[:, None] ** (i + 1.0) * sc)[:, None, :],
                                     (4, 128, CH)).astype(np.float32).copy()
    t["ret_kdec"] = np.broadcast_to((gam[:, None] ** (CH - 1.0 - i))[:, None, :],
                                    (4, 128, CH)).astype(np.float32).copy()
    DEPTH, M = 8, 256
    P = np.zeros((512, M), np.float32)
    for m in range(M):
        node = 0
        for l in range(DEPTH):
            bit = (m >> (DEPTH - 1 - l)) & 1
            col = (2 ** l - 1) + node
            P[col if bit else col + 256, m] = 1.0
            node = node * 2 + bit
    t["treeP"] = P
    t["ident"] = np.eye(128, dtype=np.float32)
    t["ident2"] = (2.0 * np.eye(128)).astype(np.float32)
    jj = np.arange(DC)
    keepT = (jj[None, :] >= jj[:, None])
    t["maddT"] = np.where(keepT, 0.0, NEG).astype(np.float32)
    t["madd_low"] = np.where(keepT.T, 0.0, -NEG).astype(np.float32)
    t["ones_row"] = np.ones((1, 128), np.float32)
    t["ones_col"] = np.ones((128, 1), np.float32)
    t["one"] = np.ones((1, 1), np.float32)
    return t


def build_program():
    nc = bass.Bass("TRN2", target_bir_lowering=False, debug=False)
    ap = {}

    def din(name, shape, dt=F32):
        ap[name] = nc.dram_tensor(name, list(shape), dt, kind="ExternalInput").ap()

    din("x_in", (D, TC))
    din("l0a_wq", (D, 1024)); din("l0a_wk", (D, 1024))
    din("l0a_wv", (D, 2048)); din("l0a_wg", (D, 2048))
    din("l0a_wo", (2048, D)); din("l0a_gnw", (512, 1))
    din("l0f_wq", (D, D)); din("l0f_rt", (D, 256))
    din("l0f_lv", (256, D)); din("l0f_wo", (D, D)); din("l0f_wb", (D, 1))
    din("l1a_win", (D, 3072)); din("l1a_wab", (D, 8)); din("l1a_wog", (D, D))
    din("l1a_bab", (8, 1)); din("l1a_bog", (D, 1))
    din("l1a_cw", (3, D, 4))
    din("l1a_wout", (D, D)); din("l1a_bout", (D, 1)); din("l1a_ramp", (D, 1))
    din("l1f_w1", (D, 4096)); din("l1f_cw", (4096, 3)); din("l1f_cb", (4096, 1))
    din("l1f_w2", (4096, D))
    din("rope_cos", (128, TC)); din("rope_sin", (128, TC))
    din("ret_maskT", (4, CH, CH)); din("ret_cross", (4, 128, CH)); din("ret_kdec", (4, 128, CH))
    din("treeP", (512, 256))
    din("ident", (128, 128)); din("ident2", (128, 128))
    din("maddT", (128, 128)); din("madd_low", (128, 128))
    din("ones_row", (1, 128)); din("ones_col", (128, 1)); din("one", (1, 1))
    din("is_sec", (128, 1)); din("is_fst", (128, 1))
    din("cvals", (128, 3))

    ap["out"] = nc.dram_tensor("out", [D, TC], F32, kind="ExternalOutput").ap()

    def dsc(name, shape, dt):
        ap[name] = nc.dram_tensor(name, list(shape), dt).ap()

    dsc("x_res", (D, TC), F32)
    dsc("o_sp", (2048, TC), BF16)
    dsc("qt_sp", (D, TC), BF16)
    dsc("reads_sp", (TC, D), BF16)
    dsc("h_sp", (4096, TC), BF16)
    dsc("q_sp", (D, TC), BF16); dsc("k_sp", (D, TC), BF16); dsc("v_sp", (D, TC), BF16)
    dsc("og_sp", (D, TC), BF16)
    dsc("tm_sp", (NDC * 4 * 128, 128), BF16)
    dsc("wt_sp", (NDC * 4 * 128, 128), BF16)
    dsc("ktd_sp", (4 * NDC * 128, 256), BF16)
    dsc("ab_sp", (8, TC), F32); dsc("lg_sp", (4, TC), F32)
    dsc("sndA", (D, CH), BF16); dsc("rcvA", (2, D, CH), BF16)
    dsc("sndA2", (D, 3), F32); dsc("rcvA2", (2, D, 3), F32)
    dsc("sndB", (D, 256), BF16); dsc("rcvB", (2, D, 256), BF16)
    dsc("sndC", (4096, 2), F32); dsc("rcvC", (2, 4096, 2), F32)

    import contextlib
    es = contextlib.ExitStack()
    sb = {}

    def S(name, shape, dt):
        def rec(nm, shp):
            if len(shp) == 2:
                return es.enter_context(nc.sbuf_tensor(nm, list(shp), dt))
            return [rec(f"{nm}_{i}", shp[1:]) for i in range(shp[0])]
        sb[name] = rec(name, list(shape))

    with es:
        S("ident_f", (128, 128), F32); S("ident_b", (128, 128), BF16)
        S("ident2_f", (128, 128), F32)
        S("maddT_f", (128, 128), F32); S("madl_f", (128, 128), F32)
        S("ones_row_f", (1, 128), F32); S("ones_col_b", (128, 1), BF16)
        S("one_f", (1, 1), F32)
        S("is_sec_c", (128, 1), F32); S("is_fst_c", (128, 1), F32)
        S("gnw_c", (4, 128, 1), F32); S("l0f_wb_c", (8, 128, 1), F32)
        S("bab_c", (8, 1), F32); S("bog_c", (8, 128, 1), F32)
        S("bout_c", (8, 128, 1), F32); S("ramp_c", (8, 128, 1), F32)
        S("rsig", (8, 128, 1), F32)
        S("cw3", (3, 8, 128, 4), F32)
        S("l1f_cw_c", (32, 128, 3), F32); S("l1f_cb_c", (32, 128, 1), F32)
        S("xf", (KC, 128, CH), F32); S("xb", (KC, 128, CH), BF16)
        S("xn", (KC, 128, CH), BF16)
        S("row", (1, CH), F32); S("row2", (1, CH), F32); S("row3", (1, CH), F32)
        S("bc1", (128, CH), F32); S("bc2", (128, CH), F32)
        S("sq", (128, CH), BF16)
        S("ao", (KC, 128, CH), BF16)
        S("tmp", (128, CH), F32); S("tmpg", (128, CH), F32)
        S("tmpb", (128, CH), BF16); S("tmpb2", (128, CH), BF16)
        S("ii32", (128, CH), I32)
        S("Zst", (4, 2, 128, 256), F32); S("Zin", (4, 2, 128, 256), F32)
        S("Zbf", (2, 128, 256), BF16)
        S("lac", (128, 1), F32); S("lacb", (128, 1), F32)
        S("acol", (128, 1), F32); S("bcol", (128, 1), F32)
        S("bacol", (128, 1), F32); S("accol", (128, 1), F32); S("dccol", (128, 1), F32)
        S("arow", (128, 1), F32); S("lrow", (1, 128), F32)
        S("lgrow", (1, 128), F32); S("brow", (1, 128), F32)
        S("s1d", (128, 1), F32); S("s2d", (128, 1), F32); S("sqc", (128, 1), F32)
        S("abr_c", (8, CH), F32); S("lga_c", (4, CH), F32)
        S("gpt", (128, 128), BF16); S("gpu", (128, 128), BF16)
        S("mmt", (128, 128), BF16)
        S("nx", (128, 128), BF16); S("nxt", (128, 128), BF16)
        S("nx2", (128, 128), BF16); S("nxt2", (128, 128), BF16)
        S("zz", (128, 128), BF16)
        S("m128", (128, 128), F32); S("m128b", (128, 128), F32)
        S("vtmt", (128, 256), BF16); S("ub", (128, 256), BF16)
        S("rhs_b", (128, 256), BF16)
        S("reads", (128, 256), F32); S("readb", (128, 256), BF16)
        S("hedge", (32, 128, 2), BF16); S("hhal", (32, 128, 2), F32)
        S("cvals_sb", (128, 3), F32)

        PS = [es.enter_context(nc.psum_tensor(f"ps{i}", [128, 512], F32)) for i in range(8)]

        sems = {}
        for e in ENGINES:
            sems[e] = es.enter_context(nc.semaphore("s_" + e))
        for e in ("sync", "gpsimd"):
            for i in range(16):
                sems[f"q_{e}{i}"] = es.enter_context(nc.semaphore(f"q_{e}{i}"))
        sems["cc"] = es.enter_context(nc.semaphore("s_cc"))

        blk = es.enter_context(nc.Block())
        nc.const_aps.register(F32, 1e-5, sb["cvals_sb"][:, 0:1])
        nc.const_aps.register(F32, 1e-6, sb["cvals_sb"][:, 1:2])
        nc.const_aps.register(F32, 1.0 / 256.0, sb["cvals_sb"][:, 2:3])
        p = Prog(nc, sems)
        c = Ctx(p, ap, sb, PS, nc)
        emit_all(c)
        p.finalize(blk)
    return nc


def pre_ln(c, n, src, deps=(), want_f32=True):
    sb = c.sb
    sl = slice(n * CH, (n + 1) * CH)
    ldf, ldb = [], []
    first = list(deps)
    for kc in range(KC):
        hsl = c.ap[src][kc * 128:(kc + 1) * 128, sl]
        if want_f32:
            ldf.append(c.load(sb["xf"][kc], hsl, deps=first, q="gpsimd"))
            first = []
        ldb.append(c.load(sb["xb"][kc], hsl, deps=first, q="gpsimd"))
        first = []
    t_sum = t_sq = None
    for kc in range(KC):
        t_sum = c.mm(6, sb["ones_col_b"], sb["xb"][kc], kc == 0, kc == KC - 1,
                     deps=[ldb[kc]], rows=1, cols=CH)
    sq_dep = list(deps)
    for kc in range(KC):
        tq = c.act(sb["sq"], sb["xb"][kc], AF.Square, deps=[ldb[kc]] + sq_dep)
        t_sq = c.mm(7, sb["ones_col_b"], sb["sq"], kc == 0, kc == KC - 1,
                    deps=[tq], rows=1, cols=CH)
        sq_dep = [t_sq]
    row, row2, row3 = sb["row"], sb["row2"], sb["row3"]
    t1 = c.ts(row, c.PS[6][0:1, :], 1.0 / D, None, OP.mult, deps=[t_sum])
    c.consume(6, t1)
    t2 = c.ts(row2, c.PS[7][0:1, :], 1.0 / D, None, OP.mult, deps=[t_sq])
    c.consume(7, t2)
    t3 = c.tt(row3, row, row, OP.mult, deps=[t1])
    t4 = c.tt(row2, row2, row3, OP.subtract, deps=[t2, t3])
    t5a = c.act(row2, row2, AF.Sqrt, bias=1e-5, deps=[t4])
    t5 = c.recip(row2, row2, deps=[t5a])
    t6 = c.tt(row3, row, row2, OP.mult, deps=[t5])
    tb1 = c.bcast(6, row2, [t5])
    tb2 = c.bcast(7, row3, [t6])
    e1 = c.cp(sb["bc1"], c.PS[6][:, :], deps=[tb1])
    c.consume(6, e1)
    e2 = c.cp(sb["bc2"], c.PS[7][:, :], deps=[tb2])
    c.consume(7, e2)
    tns = []
    for kc in range(KC):
        eng = "vector" if kc % 2 == 0 else "gpsimd"
        tb = sb["tmp"] if eng == "vector" else sb["tmpg"]
        ta = c.tt(tb, sb["xb"][kc], sb["bc1"], OP.mult, deps=[e1, ldb[kc]], eng=eng)
        tn = c.tt(sb["xn"][kc], tb, sb["bc2"], OP.subtract, deps=[ta, e2], eng=eng)
        tns.append(tn)
    return ldf, ldb, tns


def post_ln_residual(c, n, ylist, y_deps, dst):
    sb = c.sb
    sl = slice(n * CH, (n + 1) * CH)
    t_sum = t_sq = None
    for kc in range(KC):
        t_sum = c.mm(6, sb["ones_col_b"], ylist[kc], kc == 0, kc == KC - 1,
                     deps=[y_deps[kc]], rows=1, cols=CH)
    sq_dep = []
    for kc in range(KC):
        tq = c.act(sb["sq"], ylist[kc], AF.Square, deps=[y_deps[kc]] + sq_dep)
        t_sq = c.mm(7, sb["ones_col_b"], sb["sq"], kc == 0, kc == KC - 1,
                    deps=[tq], rows=1, cols=CH)
        sq_dep = [t_sq]
    row, row2, row3 = sb["row"], sb["row2"], sb["row3"]
    t1 = c.ts(row, c.PS[6][0:1, :], 1.0 / D, None, OP.mult, deps=[t_sum])
    c.consume(6, t1)
    t2 = c.ts(row2, c.PS[7][0:1, :], 1.0 / D, None, OP.mult, deps=[t_sq])
    c.consume(7, t2)
    t3 = c.tt(row3, row, row, OP.mult, deps=[t1])
    t4 = c.tt(row2, row2, row3, OP.subtract, deps=[t2, t3])
    t5a = c.act(row2, row2, AF.Sqrt, bias=1e-5, deps=[t4])
    t5 = c.recip(row2, row2, deps=[t5a])
    t6 = c.tt(row3, row, row2, OP.mult, deps=[t5])
    tb1 = c.bcast(6, row2, [t5])
    tb2 = c.bcast(7, row3, [t6])
    e1 = c.cp(sb["bc1"], c.PS[6][:, :], deps=[tb1])
    c.consume(6, e1)
    e2 = c.cp(sb["bc2"], c.PS[7][:, :], deps=[tb2])
    c.consume(7, e2)
    outticks = []
    for kc in range(KC):
        eng = "vector" if kc % 2 == 0 else "gpsimd"
        tb = sb["tmp"] if eng == "vector" else sb["tmpg"]
        ta = c.tt(tb, ylist[kc], sb["bc1"], OP.mult, deps=[e1, y_deps[kc]], eng=eng)
        tc_ = c.tt(tb, tb, sb["bc2"], OP.subtract, deps=[ta, e2], eng=eng)
        td = c.tt(sb["xf"][kc], sb["xf"][kc], tb, OP.add, deps=[tc_], eng=eng)
        st = c.store(c.ap[dst][kc * 128:(kc + 1) * 128, sl], sb["xf"][kc], deps=[td])
        outticks.append(st)
    return outticks


def l0_attn(c, CST):
    sb, p = c.sb, c.p
    with c.phase() as es:
        c.alloc(es, "rope_cos_b", (128, TC), BF16)
        c.alloc(es, "rope_sin_b", (128, TC), BF16)
        c.alloc(es, "maskT", (4, 4, 128, CH), BF16)
        c.alloc(es, "crossb", (4, 128, CH), BF16)
        c.alloc(es, "kdecb", (4, 128, CH), BF16)
        c.alloc(es, "qraw", (2, 128, CH), BF16)
        c.alloc(es, "kraw", (2, 128, CH), BF16)
        c.alloc(es, "qf", (2, 128, CH), BF16)
        c.alloc(es, "kf", (2, 128, CH), BF16)
        c.alloc(es, "qt", (2, 128, CH), BF16)
        c.alloc(es, "ktd", (2, 128, CH), BF16)
        c.alloc(es, "ktm", (4, 128, 256), BF16)
        c.alloc(es, "vtm", (4, 128, CH), BF16)
        c.alloc(es, "aT", (4, 128, CH), BF16)
        c.alloc(es, "Sbf", (2, 128, CH), BF16)
        c.alloc(es, "oev", (2, 128, CH), BF16)
        c.alloc(es, "Sst", (4, 2, 128, CH), F32)
        c.alloc(es, "WqG", (2, KC, 128, 256), BF16)
        c.alloc(es, "WkG", (2, KC, 128, 256), BF16)
        c.alloc(es, "WvG", (1, KC, 128, 512), BF16)
        cl = []
        cl.append(c.load(sb["rope_cos_b"][:], c.ap["rope_cos"]))
        cl.append(c.load(sb["rope_sin_b"][:], c.ap["rope_sin"]))
        for h in range(4):
            for js in range(4):
                cl.append(c.load(sb["maskT"][h][js],
                                 c.ap["ret_maskT"][h][js * 128:(js + 1) * 128, :]))
            cl.append(c.load(sb["crossb"][h], c.ap["ret_cross"][h]))
            cl.append(c.load(sb["kdecb"][h], c.ap["ret_kdec"][h]))
        CL = cl[-1]
        for h in range(4):
            for dkc in range(2):
                p.emit("vector", "memset", sb["Sst"][h][dkc][:, :], 0.0, deps=[])
        p.barrier()

        wg_prev = {}
        for n in range(NCH):
            sl = slice(n * CH, (n + 1) * CH)
            ldf, ldb, tns = pre_ln(c, n, "x_in", deps=[CST, CL])
            for h in range(4):
                pr = h % 2
                wts = []
                first = wg_prev.get(pr, []) + wg_prev.get("v", [])
                for kc in range(KC):
                    wts.append(c.load(sb["WqG"][pr][kc],
                                      c.ap["l0a_wq"][kc * 128:(kc + 1) * 128,
                                                     h * 256:(h + 1) * 256],
                                      deps=first))
                    first = []
                    wts.append(c.load(sb["WkG"][pr][kc],
                                      c.ap["l0a_wk"][kc * 128:(kc + 1) * 128,
                                                     h * 256:(h + 1) * 256],
                                      deps=[]))
                    wts.append(c.load(sb["WvG"][0][kc],
                                      c.ap["l0a_wv"][kc * 128:(kc + 1) * 128,
                                                     h * 512:(h + 1) * 512],
                                      deps=[]))
                WT_ = wts[-1]
                qk = {}
                lastw = []
                for (nm, W, bank) in (("q", "WqG", 0), ("k", "WkG", 1)):
                    for fc in range(2):
                        t = None
                        for kc in range(KC):
                            t = c.mm(bank, sb[W][pr][kc][:, fc * 128:(fc + 1) * 128],
                                     sb["xn"][kc], kc == 0, kc == KC - 1,
                                     deps=[tns[kc], WT_])
                        lastw.append(t)
                        dst = sb["qraw" if nm == "q" else "kraw"][fc]
                        e = c.cp(dst, c.PS[bank][:, :], deps=[t],
                                 eng="vector" if nm == "q" else "gpsimd")
                        c.consume(bank, e)
                        qk[(nm, fc)] = e
                vt = []
                for ts_ in range(4):
                    t = None
                    for kc in range(KC):
                        t = c.mm(3, sb["xn"][kc][:, ts_ * 128:(ts_ + 1) * 128],
                                 sb["WvG"][0][kc], kc == 0, kc == KC - 1,
                                 deps=[tns[kc], WT_])
                    lastw.append(t)
                    e = c.cp(sb["vtm"][ts_], c.PS[3][:, :], deps=[t], eng="gpsimd")
                    c.consume(3, e)
                    vt.append(e)
                wg_prev[pr] = lastw
                wg_prev["v"] = lastw
                cosv = sb["rope_cos_b"][:, sl]
                sinv = sb["rope_sin_b"][:, sl]
                rop = {}
                for (nm, raw, dst) in (("q", "qraw", "qf"), ("k", "kraw", "kf")):
                    d0 = [qk[(nm, 0)], qk[(nm, 1)]]
                    eng = "vector" if nm == "q" else "gpsimd"
                    tb, tb2 = ((sb["tmpb"], sb["tmpb2"]) if eng == "vector"
                               else (sb["sq"], sb["tmpg"]))
                    a1 = c.tt(tb, sb[raw][0], cosv, OP.mult, deps=d0, eng=eng)
                    a2 = c.tt(tb2, sb[raw][1], sinv, OP.mult, deps=d0, eng=eng)
                    r0 = c.tt(sb[dst][0], tb, tb2, OP.subtract, deps=[a1, a2], eng=eng)
                    a3 = c.tt(tb, sb[raw][1], cosv, OP.mult, deps=[r0], eng=eng)
                    a4 = c.tt(tb2, sb[raw][0], sinv, OP.mult, deps=[r0], eng=eng)
                    r1 = c.tt(sb[dst][1], tb, tb2, OP.add, deps=[a3, a4], eng=eng)
                    rop[nm] = (r0, r1)
                qtt, ktt = [], []
                for fc in range(2):
                    qtt.append(c.tt(sb["qt"][fc], sb["qf"][fc],
                                    sb["crossb"][h], OP.mult,
                                    deps=[rop["q"][fc]] + wg_prev.get("qtst", [])))
                    ktt.append(c.tt(sb["ktd"][fc], sb["kf"][fc],
                                    sb["kdecb"][h], OP.mult, deps=[rop["k"][fc]],
                                    eng="gpsimd"))
                qtst = []
                for fc in range(2):
                    qtst.append(c.store(
                        c.ap["qt_sp"][h * 256 + fc * 128: h * 256 + (fc + 1) * 128, sl],
                        sb["qt"][fc], deps=[qtt[fc]]))
                wg_prev["qtst"] = qtst
                ktmt = []
                for js in range(4):
                    t0_ = c.mm(4, sb["ktd"][0][:, js * 128:(js + 1) * 128], sb["ident_b"],
                               True, True, deps=[ktt[0]], cols=128, coff=0)
                    t1_ = c.mm(4, sb["ktd"][1][:, js * 128:(js + 1) * 128], sb["ident_b"],
                               True, True, deps=[ktt[1]], cols=128, coff=128)
                    e = c.cp(sb["ktm"][js], c.PS[4][:, 0:256], deps=[t0_, t1_],
                             eng="gpsimd")
                    c.consume(4, e)
                    ktmt.append(e)
                att = []
                for js in range(4):
                    t = None
                    for fc in range(2):
                        t = c.mm(2, sb["kf"][fc][:, js * 128:(js + 1) * 128], sb["qf"][fc],
                                 fc == 0, fc == 1, deps=[rop["k"][fc], rop["q"][fc]])
                    e = c.tt(sb["aT"][js], c.PS[2][:, :], sb["maskT"][h][js],
                             OP.mult, deps=[t])
                    c.consume(2, e)
                    att.append(e)
                sbf = []
                for dkc in range(2):
                    e = c.cp(sb["Sbf"][dkc], sb["Sst"][h][dkc],
                             deps=wg_prev.get("sbf", []), eng="gpsimd")
                    sbf.append(e)
                oprev = wg_prev.get("oev", {})
                for dvs in range(4):
                    bank = (0, 1, 3, 5)[dvs]
                    t = None
                    for js in range(4):
                        t = c.mm(bank, sb["vtm"][js][:, dvs * 128:(dvs + 1) * 128],
                                 sb["aT"][js], js == 0, False, deps=[vt[js], att[js]])
                    for dkc in range(2):
                        t = c.mm(bank, sb["Sbf"][dkc][:, dvs * 128:(dvs + 1) * 128],
                                 sb["qt"][dkc], False, dkc == 1, deps=[sbf[dkc], qtt[dkc]])
                    ev = c.cp(sb["oev"][dvs % 2], c.PS[bank][:, :],
                              deps=[t] + oprev.get(dvs % 2, []),
                              eng="vector" if dvs % 2 == 0 else "gpsimd")
                    c.consume(bank, ev)
                    st = c.store(c.ap["o_sp"][h * 512 + dvs * 128: h * 512 + (dvs + 1) * 128, sl],
                                 sb["oev"][dvs % 2], deps=[ev])
                    oprev[dvs % 2] = [st]
                    wg_prev.setdefault("sbf", []).clear() if False else None
                wg_prev["oev"] = oprev
                wg_prev["sbf"] = [t]
                for dks in range(2):
                    bank = (2, 4)[dks]
                    t = None
                    for js in range(4):
                        t = c.mm(bank, sb["ktm"][js][:, dks * 128:(dks + 1) * 128],
                                 sb["vtm"][js], js == 0, js == 3, deps=[ktmt[js], vt[js]])
                    zt = c.stt(sb["Sst"][h][dks], sb["Sst"][h][dks], GAM[h] ** CH,
                               c.PS[bank][:, :], OP.mult, OP.add, deps=[t, sbf[dks]])
                    c.consume(bank, zt)
            p.soft_barrier("l0a1")

        sends = []
        for h in range(4):
            for dkc in range(2):
                sends.append(c.store(
                    c.ap["sndA"][h * 256 + dkc * 128: h * 256 + (dkc + 1) * 128, :],
                    sb["Sst"][h][dkc], deps=[]))
        g = c.gather(c.ap["sndA"], c.ap["rcvA"], deps=sends)
        p.barrier()

    with c.phase() as es:
        c.alloc(es, "Sin", (4, 2, 128, CH), F32)
        c.alloc(es, "Sbf2", (2, 128, CH), BF16)
        c.alloc(es, "qt2", (2, 128, CH), BF16)
        c.alloc(es, "og_", (16, 128, CH), BF16)
        c.alloc(es, "gsl", (4, 128, CH), BF16)
        c.alloc(es, "Wd", (KC, 128, 2048), BF16)
        c.alloc(es, "We", (16, 128, 1024), BF16)
        first = [g]
        sl_ticks = []
        for h in range(4):
            for dkc in range(2):
                l = c.load(sb["Sin"][h][dkc],
                           c.ap["rcvA"][0][h * 256 + dkc * 128: h * 256 + (dkc + 1) * 128, :],
                           deps=first, q="gpsimd")
                first = []
                sl_ticks.append(l)
        for h in range(4):
            for dkc in range(2):
                c.ts(sb["Sin"][h][dkc], sb["Sin"][h][dkc], sb["is_sec_c"][:, 0:1],
                     None, OP.mult, deps=[sl_ticks[-1]])
        for kc in range(KC):
            c.load(sb["Wd"][kc], c.ap["l0a_wg"][kc * 128:(kc + 1) * 128, :])
        for dvc in range(16):
            c.load(sb["We"][dvc], c.ap["l0a_wo"][dvc * 128:(dvc + 1) * 128, :])
        p.barrier()

        for n in range(NCH):
            sl = slice(n * CH, (n + 1) * CH)
            ldf, ldb, tns = pre_ln(c, n, "x_in", deps=[])
            ognt = {}
            p2war = []
            for h in range(4):
                qtl = []
                first = list(p2war)
                for fc in range(2):
                    qtl.append(c.load(sb["qt2"][fc],
                                      c.ap["qt_sp"][h * 256 + fc * 128: h * 256 + (fc + 1) * 128, sl],
                                      deps=first, q="sync"))
                    first = []
                sbf = []
                for dkc in range(2):
                    sbf.append(c.cp(sb["Sbf2"][dkc], sb["Sin"][h][dkc], deps=p2war,
                                    eng="gpsimd"))
                opl = []
                for dvs in range(4):
                    opl.append(c.load(sb["og_"][h * 4 + dvs],
                                      c.ap["o_sp"][h * 512 + dvs * 128: h * 512 + (dvs + 1) * 128, sl],
                                      deps=[], q="sync"))
                odep = []
                for dvs in range(4):
                    bank = (0, 1, 3, 5)[dvs]
                    t = None
                    for dkc in range(2):
                        t = c.mm(bank, sb["Sbf2"][dkc][:, dvs * 128:(dvs + 1) * 128],
                                 sb["qt2"][dkc], dkc == 0, dkc == 1,
                                 deps=[sbf[dkc], qtl[dkc]])
                    e = c.tt(sb["og_"][h * 4 + dvs], c.PS[bank][:, :],
                             sb["og_"][h * 4 + dvs], OP.add, deps=[t, opl[dvs]])
                    c.consume(bank, e)
                    odep.append(e)
                    p2war = [t]
                if n < NCH - 1:
                    for dkc in range(2):
                        c.ts(sb["Sin"][h][dkc], sb["Sin"][h][dkc], GAM[h] ** CH, None,
                             OP.mult, deps=[sbf[dkc]], eng="gpsimd")
                t_sq = None
                sqd = []
                for dvs in range(4):
                    tq = c.act(sb["sq"], sb["og_"][h * 4 + dvs], AF.Square,
                               deps=[odep[dvs]] + sqd)
                    t_sq = c.mm(6, sb["ones_col_b"], sb["sq"], dvs == 0, dvs == 3,
                                deps=[tq], rows=1, cols=CH)
                    sqd = [t_sq]
                t1 = c.ts(sb["row"], c.PS[6][0:1, :], 1.0 / 512.0, None, OP.mult,
                          deps=[t_sq])
                c.consume(6, t1)
                t2a = c.act(sb["row"], sb["row"], AF.Sqrt, bias=1e-5, deps=[t1])
                t2 = c.recip(sb["row"], sb["row"], deps=[t2a])
                tb = c.bcast(6, sb["row"], [t2])
                e1 = c.cp(sb["bc1"], c.PS[6][:, :], deps=[tb])
                c.consume(6, e1)
                for dvs in range(4):
                    bank = (2, 4)[dvs % 2]
                    t = None
                    for kc in range(KC):
                        t = c.mm(bank,
                                 sb["Wd"][kc][:, h * 512 + dvs * 128: h * 512 + (dvs + 1) * 128],
                                 sb["xn"][kc], kc == 0, kc == KC - 1, deps=[tns[kc]])
                    gs = c.act(sb["gsl"][dvs], c.PS[bank][:, :], AF.Silu, deps=[t])
                    c.consume(bank, gs)
                    o1 = c.stt(sb["og_"][h * 4 + dvs], sb["og_"][h * 4 + dvs],
                               sb["gnw_c"][dvs][:, 0:1], sb["bc1"],
                               OP.mult, OP.mult, deps=[odep[dvs], e1])
                    ognt[h * 4 + dvs] = c.tt(sb["og_"][h * 4 + dvs],
                                             sb["og_"][h * 4 + dvs],
                                             sb["gsl"][dvs], OP.mult, deps=[o1, gs])
            ytick, ylist = [], []
            for ms in range(8):
                bank = (0, 1, 3, 5)[ms % 4]
                t = None
                for dvc in range(16):
                    t = c.mm(bank, sb["We"][dvc][:, ms * 128:(ms + 1) * 128],
                             sb["og_"][dvc], dvc == 0, dvc == 15, deps=[ognt[dvc]])
                e = c.cp(sb["ao"][ms], c.PS[bank][:, :], deps=[t],
                         eng="vector" if ms % 2 == 0 else "gpsimd")
                c.consume(bank, e)
                ytick.append(e)
                ylist.append(sb["ao"][ms])
            post_ln_residual(c, n, ylist, ytick, "x_res")
            p.soft_barrier("l0a2")
        p.barrier()


def l0_ffn(c):
    sb, p = c.sb, c.p
    with c.phase() as es:
        c.alloc(es, "Wq0", (KC, 128, 1024), BF16)
        c.alloc(es, "rtw", (KC, 128, 256), BF16)
        c.alloc(es, "Wo0", (KC, 128, 1024), BF16)
        c.alloc(es, "lvw", (2, 128, 1024), BF16)
        c.alloc(es, "treeP_b", (4, 128, 256), BF16)
        c.alloc(es, "qful", (KC, 128, CH), BF16)
        c.alloc(es, "u_b", (4, 128, CH), BF16)
        c.alloc(es, "lp_b", (2, 128, CH), BF16)
        for kc in range(KC):
            c.load(sb["Wq0"][kc], c.ap["l0f_wq"][kc * 128:(kc + 1) * 128, :])
            c.load(sb["rtw"][kc], c.ap["l0f_rt"][kc * 128:(kc + 1) * 128, :])
            c.load(sb["Wo0"][kc], c.ap["l0f_wo"][kc * 128:(kc + 1) * 128, :])
        for mc in range(2):
            c.load(sb["lvw"][mc], c.ap["l0f_lv"][mc * 128:(mc + 1) * 128, :])
        for uc in range(4):
            c.load(sb["treeP_b"][uc], c.ap["treeP"][uc * 128:(uc + 1) * 128, :])
        p.barrier()

        for n in range(NCH):
            ldf, ldb, tns = pre_ln(c, n, "x_res", deps=[])
            qt = []
            for fc in range(KC):
                bank = (0, 1)[fc % 2]
                t = None
                for kc in range(KC):
                    t = c.mm(bank, sb["Wq0"][kc][:, fc * 128:(fc + 1) * 128], sb["xn"][kc],
                             kc == 0, kc == KC - 1, deps=[tns[kc]])
                e = c.cp(sb["qful"][fc], c.PS[bank][:, :], deps=[t],
                         eng="vector" if fc % 2 == 0 else "gpsimd")
                c.consume(bank, e)
                qt.append(e)
            lg = []
            for ms2 in range(2):
                bank = (2, 3)[ms2]
                t = None
                for ec in range(KC):
                    t = c.mm(bank, sb["rtw"][ec][:, ms2 * 128:(ms2 + 1) * 128],
                             sb["qful"][ec], ec == 0, ec == KC - 1, deps=[qt[ec]])
                lg.append(t)
            prev_read = []
            udep = []
            for ms2 in range(2):
                bank = (2, 3)[ms2]
                a1 = c.act(sb["tmp"], c.PS[bank][:, :], AF.Abs, deps=[lg[ms2]] + prev_read)
                a2 = c.act(sb["tmp"], sb["tmp"], AF.Exp, scale=-1.0, deps=[a1])
                a3 = c.act(sb["tmp"], sb["tmp"], AF.Ln, bias=1.0, deps=[a2])
                a4 = c.act(sb["tmpg"], c.PS[bank][:, :], AF.Relu, deps=[lg[ms2]] + prev_read)
                s1 = c.tt(sb["tmpg"], sb["tmpg"], sb["tmp"], OP.add, deps=[a3, a4])
                s2 = c.ts(sb["u_b"][2 + ms2], sb["tmpg"], -1.0, None, OP.mult, deps=[s1])
                s3 = c.tt(sb["u_b"][ms2], c.PS[bank][:, :], sb["tmpg"], OP.subtract,
                          deps=[s1])
                c.consume(bank, s3)
                prev_read = [s1, s3, s2]
                udep += [s2, s3]
            lpt = []
            for ms2 in range(2):
                bank = (4, 5)[ms2]
                t = None
                for uc in range(4):
                    t = c.mm(bank, sb["treeP_b"][uc][:, ms2 * 128:(ms2 + 1) * 128],
                             sb["u_b"][uc], uc == 0, uc == 3, deps=udep)
                e = c.act(sb["lp_b"][ms2], c.PS[bank][:, :], AF.Exp, deps=[t])
                c.consume(bank, e)
                lpt.append(e)
            rt = []
            for ds in range(8):
                bank = (0, 1)[ds % 2]
                t = None
                for mc in range(2):
                    t = c.mm(bank, sb["lvw"][mc][:, ds * 128:(ds + 1) * 128],
                             sb["lp_b"][mc], mc == 0, mc == 1, deps=[lpt[mc], qt[ds]])
                e = c.cp(sb["qful"][ds], c.PS[bank][:, :], deps=[t],
                         eng="vector" if ds % 2 == 0 else "gpsimd")
                c.consume(bank, e)
                rt.append(e)
            ytick, ylist = [], []
            for es_ in range(8):
                bank = (2, 3)[es_ % 2]
                t = None
                for dc in range(KC):
                    t = c.mm(bank, sb["Wo0"][dc][:, es_ * 128:(es_ + 1) * 128],
                             sb["qful"][dc], dc == 0, dc == KC - 1, deps=[rt[dc]])
                e = c.ts(sb["ao"][es_], c.PS[bank][:, :],
                         sb["l0f_wb_c"][es_][:, 0:1], None, OP.add, deps=[t])
                c.consume(bank, e)
                ytick.append(e)
                ylist.append(sb["ao"][es_])
            post_ln_residual(c, n, ylist, ytick, "x_res")
            p.soft_barrier("l0f")
        p.barrier()


def l1_attn(c):
    sb, p = c.sb, c.p
    with c.phase() as es:
        c.alloc(es, "Wf", (KC, 128, 3072), BF16)
        c.alloc(es, "Wog", (KC, 128, 1024), BF16)
        c.alloc(es, "wab_s", (KC, 128, 8), BF16)
        c.alloc(es, "qkv", (24, 128, CH + 3), BF16)
        c.alloc(es, "cv", (2, 128, CH), BF16)
        c.alloc(es, "ogc", (128, CH), BF16)
        c.alloc(es, "halo_xn", (KC, 128, 3), BF16)
        for kc in range(KC):
            c.load(sb["Wf"][kc], c.ap["l1a_win"][kc * 128:(kc + 1) * 128, :])
            c.load(sb["Wog"][kc], c.ap["l1a_wog"][kc * 128:(kc + 1) * 128, :])
            c.load(sb["wab_s"][kc], c.ap["l1a_wab"][kc * 128:(kc + 1) * 128, :])
        p.barrier()

        ldf, ldb, tns = pre_ln(c, NCH - 1, "x_res", deps=[], want_f32=False)
        sends = []
        for kc in range(KC):
            sends.append(c.store(c.ap["sndA2"][kc * 128:(kc + 1) * 128, :],
                                 sb["xn"][kc][:, CH - 3:CH], deps=[tns[kc]]))
        g2 = c.gather(c.ap["sndA2"], c.ap["rcvA2"], deps=sends)
        hx = []
        first = [g2]
        hl = []
        for kc in range(KC):
            hl.append(c.load(sb["xf"][kc][:, 0:3],
                             c.ap["rcvA2"][0][kc * 128:(kc + 1) * 128, :],
                             deps=first, q="gpsimd"))
            first = []
        for kc in range(KC):
            hx.append(c.ts(sb["halo_xn"][kc], sb["xf"][kc][:, 0:3],
                           sb["is_sec_c"][:, 0:1], None, OP.mult, deps=[hl[-1]]))
        halo_pr = []
        for fs in range(24):
            t = None
            for kc in range(KC):
                t = c.mm(5, sb["Wf"][kc][:, fs * 128:(fs + 1) * 128], sb["halo_xn"][kc],
                         kc == 0, kc == KC - 1, deps=[hx[kc]], cols=3)
            e = c.cp(sb["qkv"][fs][:, 0:3], c.PS[5][:, 0:3], deps=[t], eng="gpsimd")
            c.consume(5, e)
            halo_pr.append(e)
        p.barrier()

        ablg_prev = []
        for n in range(NCH):
            t0, t1 = n * CH, (n + 1) * CH
            sl = slice(t0, t1)
            ldf, ldb, tns = pre_ln(c, n, "x_res", deps=[], want_f32=False)
            t = None
            for kc in range(KC):
                t = c.mm(5, sb["wab_s"][kc], sb["xn"][kc], kc == 0, kc == KC - 1,
                         deps=[tns[kc]], rows=8)
            aa = c.act(sb["abr_c"][0:8, :], c.PS[5][0:8, :], AF.Sigmoid,
                       bias=sb["bab_c"][0:8, 0:1], deps=[t] + ablg_prev)
            c.consume(5, aa)
            ab2 = c.ts(sb["abr_c"][0:4, :], sb["abr_c"][0:4, :], 0.98, 0.01,
                       OP.mult, OP.add, deps=[aa])
            bb = ab2
            la = c.act(sb["lga_c"][0:4, :], sb["abr_c"][0:4, :], AF.Ln,
                       deps=[ab2] + ablg_prev)
            ablg_prev = [c.store(c.ap["ab_sp"][:, sl], sb["abr_c"][0:8, :],
                                 deps=[ab2, bb]),
                         c.store(c.ap["lg_sp"][:, sl], sb["lga_c"][0:4, :], deps=[la])]
            ogst = []
            for kc8 in range(KC):
                bank = (2, 3)[kc8 % 2]
                t = None
                for kc in range(KC):
                    t = c.mm(bank, sb["Wog"][kc][:, kc8 * 128:(kc8 + 1) * 128],
                             sb["xn"][kc], kc == 0, kc == KC - 1, deps=[tns[kc]])
                e = c.act(sb["ogc"], c.PS[bank][:, :], AF.Sigmoid,
                          bias=sb["bog_c"][kc8][:, 0:1], deps=[t] + ogst)
                st = c.store(c.ap["og_sp"][kc8 * 128:(kc8 + 1) * 128, sl], sb["ogc"],
                             deps=[e])
                ogst = [st]
                c.consume(bank, st)
            cvst = {}
            prt = []
            for fs in range(24):
                bank = (0, 1)[fs % 2]
                t = None
                for kc in range(KC):
                    t = c.mm(bank, sb["Wf"][kc][:, fs * 128:(fs + 1) * 128], sb["xn"][kc],
                             kc == 0, kc == KC - 1, deps=[tns[kc]])
                e = c.cp(sb["qkv"][fs][:, 3:3 + CH], c.PS[bank][:, :],
                         deps=[t] + (halo_pr if n == 0 else []),
                         eng="vector" if fs % 2 == 0 else "gpsimd")
                c.consume(bank, e)
                prt.append(e)
            for gi, (grp, base, dstn) in enumerate((("q", 0, "q_sp"), ("k", 8, "k_sp"),
                                                    ("v", 16, "v_sp"))):
                for h in range(4):
                    sil = [None, None]
                    for fc in range(2):
                        fs = base + 2 * h + fc
                        eng = "vector" if fs % 2 == 0 else "gpsimd"
                        tf = sb["tmp"] if eng == "vector" else sb["tmpg"]

                        def wcol(tau, fs=fs, gi=gi):
                            return sb["cw3"][gi][fs % 8][:, tau:tau + 1]
                        tcv = c.ts(tf, sb["qkv"][fs][:, 0:CH], wcol(0), None, OP.mult,
                                   deps=[prt[fs]], eng=eng)
                        for tau in range(1, 4):
                            tcv = c.stt(tf, sb["qkv"][fs][:, tau:tau + CH], wcol(tau), tf,
                                        OP.mult, OP.add, deps=[tcv], eng=eng)
                        sil[fc] = c.act(sb["cv"][fc], tf, AF.Silu,
                                        deps=[tcv] + cvst.get(fc, []))
                        c.cp(sb["qkv"][fs][:, 0:3], sb["qkv"][fs][:, CH:CH + 3],
                             deps=[sil[fc]], eng=eng)
                    if grp in ("q", "k"):
                        t_sq = None
                        sqd = []
                        for fc in range(2):
                            tq = c.act(sb["sq"], sb["cv"][fc], AF.Square,
                                       deps=[sil[fc]] + sqd)
                            t_sq = c.mm(6, sb["ones_col_b"], sb["sq"], fc == 0, fc == 1,
                                        deps=[tq], rows=1, cols=CH)
                            sqd = [t_sq]
                        r1 = c.act(sb["row"], c.PS[6][0:1, :], AF.Sqrt, deps=[t_sq])
                        c.consume(6, r1)
                        r2 = c.ts(sb["row"], sb["row"], 1e-6, None, OP.max, deps=[r1])
                        r3 = c.recip(sb["row"], sb["row"], deps=[r2])
                        tb = c.bcast(6, sb["row"], [r3])
                        e1 = c.cp(sb["bc1"], c.PS[6][:, :], deps=[tb])
                        c.consume(6, e1)
                        for fc in range(2):
                            m = c.tt(sb["cv"][fc], sb["cv"][fc], sb["bc1"], OP.mult,
                                     deps=[sil[fc], e1],
                                     eng="vector" if fc == 0 else "gpsimd")
                            cvst[fc] = [c.store(
                                c.ap[dstn][h * 256 + fc * 128: h * 256 + (fc + 1) * 128, sl],
                                sb["cv"][fc], deps=[m])]
                    else:
                        for fc in range(2):
                            cvst[fc] = [c.store(
                                c.ap[dstn][h * 256 + fc * 128: h * 256 + (fc + 1) * 128, sl],
                                sb["cv"][fc], deps=[sil[fc]])]
            p.soft_barrier("l1ap")
        p.barrier()

    for es_ in range(8):
        c.act(sb["rsig"][es_], sb["ramp_c"][es_][:, 0:1], AF.Sigmoid, deps=[])

    with c.phase() as es:
        c.alloc(es, "kq", (4, 128, 128), BF16)
        prevT = []
        for ci in range(NDC):
            t0, t1 = ci * DC, (ci + 1) * DC
            for h in range(4):
                lds = []
                first = list(prevT)
                for fc in range(2):
                    lds.append(c.load(sb["kq"][fc],
                                      c.ap["k_sp"][h * 256 + fc * 128: h * 256 + (fc + 1) * 128, t0:t1],
                                      deps=first, q="sync"))
                    first = []
                    lds.append(c.load(sb["kq"][2 + fc],
                                      c.ap["q_sp"][h * 256 + fc * 128: h * 256 + (fc + 1) * 128, t0:t1],
                                      deps=[], q="sync"))
                lds = [lds[-1]] * 4
                lrl = c.load(sb["lgrow"], c.ap["lg_sp"][h:h + 1, t0:t1], deps=prevT,
                             q="gpsimd")
                brl = c.load(sb["brow"], c.ap["ab_sp"][4 + h:5 + h, t0:t1], deps=[],
                             q="gpsimd")
                lrl = brl
                s1 = c.scan(sb["lrow"], sb["lgrow"], 0.0, OP.add, deps=[lrl] + prevT)
                tA = c.rowT(0, sb["lrow"], [s1], coff=0)
                e1 = c.cp(sb["lac"], c.PS[0][:, 0:1], deps=[tA])
                tC = c.mm(0, sb["ones_row_f"][0:1, :], sb["lrow"][0:1, 127:128], True,
                          True, deps=[s1, e1], cols=1, coff=1)
                elb = c.cp(sb["lacb"], c.PS[0][:, 1:2], deps=[tC])
                c.consume(0, elb)
                tB = c.bcast(1, sb["lrow"], [s1], cols=128)
                tb_ = c.rowT(2, sb["brow"], [brl], coff=0)
                eb = c.cp(sb["bcol"], c.PS[2][:, 0:1], deps=[tb_])
                bbc = c.bcast(2, sb["brow"], [brl, eb], cols=128, coff=128)
                tkk = tkq = None
                for fc in range(2):
                    tkk = c.mm(3, sb["kq"][fc], sb["kq"][fc], fc == 0, fc == 1,
                               deps=[lds[2 * fc]], cols=128)
                    tkq = c.mm(4, sb["kq"][fc], sb["kq"][2 + fc], fc == 0, fc == 1,
                               deps=[lds[2 * fc], lds[2 * fc + 1]], cols=128)
                d1 = c.ts(sb["m128"], c.PS[1][:, 0:128], sb["lac"][:, 0:1], None,
                          OP.subtract, deps=[tB, e1])
                d2 = c.tt(sb["m128"], sb["m128"], sb["maddT_f"], OP.add, deps=[d1])
                gpt = c.act(sb["gpt"], sb["m128"], AF.Exp, deps=[d2])
                gst = c.tt(sb["tmpb"][:, 0:128], sb["gpt"], sb["ident_b"], OP.subtract,
                           deps=[gpt])
                d3 = c.ts(sb["m128b"], c.PS[1][:, 0:128], sb["lac"][:, 0:1], None,
                          OP.subtract, deps=[tB, e1])
                c.consume(1, d3)
                d4 = c.tt(sb["m128b"], sb["m128b"], sb["madl_f"], OP.add, deps=[d3])
                gpu = c.act(sb["gpu"], sb["m128b"], AF.Exp, scale=-1.0, deps=[d4])
                gsu = c.tt(sb["tmpb2"][:, 0:128], sb["gpu"], sb["ident_b"], OP.subtract,
                           deps=[gpu])
                m1 = c.tt(sb["m128"], c.PS[3][:, 0:128], sb["tmpb"][:, 0:128], OP.mult,
                          deps=[tkk, gst])
                ebb = c.cp(sb["bc1"][:, 0:128], c.PS[2][:, 128:256], deps=[bbc])
                c.consume(2, ebb)
                m2 = c.tt(sb["m128"], sb["m128"], sb["bc1"][:, 0:128], OP.mult,
                          deps=[m1, ebb])
                mmt = c.tt(sb["mmt"], sb["m128"], sb["ident_f"], OP.add, deps=[m2])
                x0t = c.tt(sb["nxt"], sb["ident_f"], sb["m128"], OP.subtract, deps=[m2])
                m3 = c.tt(sb["m128b"], c.PS[3][:, 0:128], sb["tmpb2"][:, 0:128], OP.mult,
                          deps=[tkk, gsu])
                c.consume(3, m3)
                m4 = c.ts(sb["m128b"], sb["m128b"], sb["bcol"][:, 0:1], None, OP.mult,
                          deps=[m3, eb])
                x0 = c.tt(sb["nx"], sb["ident_f"], sb["m128b"], OP.subtract, deps=[m4])
                wt_ = c.tt(sb["tmpb"][:, 0:128], c.PS[4][:, 0:128], sb["gpt"], OP.mult,
                           deps=[tkq, gpt])
                c.consume(4, wt_)
                stw = c.store(c.ap["wt_sp"][(ci * 4 + h) * 128:(ci * 4 + h + 1) * 128, :],
                              sb["tmpb"][:, 0:128], deps=[wt_])
                xa, xat = "nx", "nxt"
                xb_, xbt = "nx2", "nxt2"
                tx, txt = x0, x0t
                for it in range(6):
                    ty = c.mm(5, sb["mmt"], sb[xa], True, True, deps=[mmt, tx], cols=128)
                    tz = c.tt(sb["zz"], sb["ident2_f"], c.PS[5][:, 0:128], OP.subtract,
                              deps=[ty])
                    c.consume(5, tz)
                    tX = c.mm(5, sb[xat], sb["zz"], True, True, deps=[txt, tz], cols=128)
                    ex = c.cp(sb[xb_], c.PS[5][:, 0:128], deps=[tX])
                    c.consume(5, ex)
                    tXT = c.mm(5, sb["zz"], sb[xat], True, True, deps=[tz, txt], cols=128)
                    ext = c.cp(sb[xbt], c.PS[5][:, 0:128], deps=[tXT])
                    c.consume(5, ext)
                    xa, xb_ = xb_, xa
                    xat, xbt = xbt, xat
                    tx, txt = ex, ext
                stt_ = c.store(c.ap["tm_sp"][(ci * 4 + h) * 128:(ci * 4 + h + 1) * 128, :],
                               sb[xat], deps=[txt])
                dcc = c.act(sb["dccol"], sb["lac"], AF.Exp, scale=-1.0,
                            bias=sb["lacb"][:, 0:1], deps=[e1, elb])
                tk0 = c.mm(0, sb["kq"][0], sb["ident_b"], True, True, deps=[elb],
                           cols=128, coff=0)
                tk1 = c.mm(0, sb["kq"][1], sb["ident_b"], True, True, deps=[],
                           cols=128, coff=128)
                ek = c.ts(sb["tmpb2"][:, 0:256], c.PS[0][:, 0:256], sb["dccol"][:, 0:1],
                          None, OP.mult, deps=[tk0, tk1, dcc])
                c.consume(0, ek)
                stk = c.store(c.ap["ktd_sp"][(h * NDC + ci) * 128:(h * NDC + ci + 1) * 128, :],
                              sb["tmpb2"][:, 0:256], deps=[ek])
                prevT = [tA, tC, tB, tb_, stw, stt_, stk, s1, ek, wt_, gst, gsu, d4]
            p.soft_barrier("l1tm", lag=4)
        p.barrier()

    for h in range(4):
        for kc2 in range(2):
            p.emit("vector", "memset", sb["Zst"][h][kc2][:, :], 0.0, deps=[])
    p.barrier()
    l1_sweep(c, "Zst", correction=False)
    p.barrier()

    sends = []
    for h in range(4):
        for kc2 in range(2):
            sends.append(c.store(
                c.ap["sndB"][h * 256 + kc2 * 128: h * 256 + (kc2 + 1) * 128, :],
                sb["Zst"][h][kc2], deps=[]))
    g = c.gather(c.ap["sndB"], c.ap["rcvB"], deps=sends)
    first = [g]
    zl_ticks = []
    for h in range(4):
        for kc2 in range(2):
            l = c.load(sb["Zin"][h][kc2],
                       c.ap["rcvB"][0][h * 256 + kc2 * 128: h * 256 + (kc2 + 1) * 128, :],
                       deps=first, q="gpsimd")
            first = []
            zl_ticks.append(l)
    for h in range(4):
        for kc2 in range(2):
            c.ts(sb["Zin"][h][kc2], sb["Zin"][h][kc2], sb["is_sec_c"][:, 0:1],
                 None, OP.mult, deps=[zl_ticks[-1]])
    p.barrier()
    l1_sweep(c, "Zin", correction=True)
    p.barrier()


def l1_sweep(c, zname, correction):
    sb, p = c.sb, c.p
    with c.phase() as es:
        c.alloc(es, "kq", (4, 128, 128), BF16)
        c.alloc(es, "vfc", (2, 128, 128), BF16)
        c.alloc(es, "tmT", (128, 128), BF16)
        c.alloc(es, "wT", (128, 128), BF16)
        c.alloc(es, "ktdc", (128, 256), BF16)
        c.alloc(es, "rfm", (KC, 128, CH), BF16)
        if correction:
            c.alloc(es, "Wout", (KC, 128, 1024), BF16)
            c.alloc(es, "ogg", (KC, 128, CH), BF16)
            for kc in range(KC):
                c.load(sb["Wout"][kc], c.ap["l1a_wout"][kc * 128:(kc + 1) * 128, :])
        p.barrier()
        rfm_t = {}
        carry = {}
        for ci in range(NDC):
            t0, t1 = ci * DC, (ci + 1) * DC
            for h in range(4):
                pv = carry.get("ld", [])
                lds = []
                first = list(pv)
                for fc in range(2):
                    lds.append(c.load(sb["kq"][fc],
                                      c.ap["k_sp"][h * 256 + fc * 128: h * 256 + (fc + 1) * 128, t0:t1],
                                      deps=first, q="sync"))
                    first = []
                    lds.append(c.load(sb["kq"][2 + fc],
                                      c.ap["q_sp"][h * 256 + fc * 128: h * 256 + (fc + 1) * 128, t0:t1],
                                      deps=[], q="sync"))
                ltm = c.load(sb["tmT"],
                             c.ap["tm_sp"][(ci * 4 + h) * 128:(ci * 4 + h + 1) * 128, :],
                             deps=[], q="sync")
                lwt = c.load(sb["wT"],
                             c.ap["wt_sp"][(ci * 4 + h) * 128:(ci * 4 + h + 1) * 128, :],
                             deps=[], q="sync")
                lkt = c.load(sb["ktdc"],
                             c.ap["ktd_sp"][(h * NDC + ci) * 128:(h * NDC + ci + 1) * 128, :],
                             deps=[], q="sync")
                lds = [lkt] * 4
                ltm = lwt = lkt
                lrl = c.load(sb["lgrow"], c.ap["lg_sp"][h:h + 1, t0:t1],
                             deps=carry.get("rowr", []), q="gpsimd")
                brl = c.load(sb["brow"], c.ap["ab_sp"][4 + h:5 + h, t0:t1],
                             deps=[], q="gpsimd")
                lrl = brl
                s1 = c.scan(sb["lrow"], sb["lgrow"], 0.0, OP.add,
                            deps=[lrl] + carry.get("lrowr", []))
                tA = c.rowT(6, sb["lrow"], [s1], coff=0)
                e1 = c.cp(sb["lac"], c.PS[6][:, 0:1], deps=[tA] + carry.get("lacr", []))
                tC = c.mm(6, sb["ones_row_f"][0:1, :], sb["lrow"][0:1, 127:128], True,
                          True, deps=[s1, e1], cols=1, coff=1)
                elb = c.cp(sb["lacb"], c.PS[6][:, 1:2], deps=[tC])
                tb_ = c.rowT(6, sb["brow"], [brl, elb], coff=2)
                eb = c.cp(sb["bcol"], c.PS[6][:, 2:3], deps=[tb_] + carry.get("bcolr", []))
                c.consume(6, eb)
                a1 = c.act(sb["acol"], sb["lac"], AF.Exp, deps=[e1] + carry.get("acolr", []))
                a2 = c.act(sb["accol"], sb["lacb"], AF.Exp,
                           deps=[elb] + carry.get("accolr", []))
                carry["rowr"] = [tA, tC, tb_]
                carry["lrowr"] = [tA, tC]
                carry["lacr"] = [a1]
                carry["bcolr"] = []
                carry["accolr"] = []
                Z = sb[zname][h]
                zbf = []
                for kc2 in range(2):
                    zbf.append(c.cp(sb["Zbf"][kc2], Z[kc2],
                                    deps=carry.get(("z", h, kc2), []) + pv,
                                    eng="gpsimd"))
                tkz = tqz = None
                for kc2 in range(2):
                    tkz = c.mm(0, sb["kq"][kc2], sb["Zbf"][kc2], kc2 == 0, kc2 == 1,
                               deps=[zbf[kc2], lds[2 * kc2]], cols=256)
                    tqz = c.mm(1, sb["kq"][2 + kc2], sb["Zbf"][kc2], kc2 == 0, kc2 == 1,
                               deps=[zbf[kc2], lds[2 * kc2 + 1]], cols=256)
                if not correction:
                    lv0 = c.load(sb["vfc"][0],
                                 c.ap["v_sp"][h * 256:h * 256 + 128, t0:t1],
                                 deps=pv, q="sync")
                    lv1 = c.load(sb["vfc"][1],
                                 c.ap["v_sp"][h * 256 + 128:h * 256 + 256, t0:t1],
                                 deps=[], q="sync")
                    lv0 = lv1
                    tv0 = c.mm(2, sb["vfc"][0], sb["ident_b"], True, True, deps=[lv0],
                               cols=128, coff=0)
                    tv1 = c.mm(2, sb["vfc"][1], sb["ident_b"], True, True, deps=[lv1],
                               cols=128, coff=128)
                    ev = c.cp(sb["vtmt"], c.PS[2][:, 0:256], deps=[tv0, tv1],
                              eng="gpsimd")
                    c.consume(2, ev)
                    r1 = c.ts(sb["tmp"][:, 0:256], c.PS[0][:, 0:256], sb["acol"][:, 0:1],
                              None, OP.mult, deps=[tkz, a1])
                    c.consume(0, r1)
                    r2 = c.tt(sb["tmp"][:, 0:256], sb["vtmt"], sb["tmp"][:, 0:256],
                              OP.subtract, deps=[r1, ev])
                    r3 = c.ts(sb["rhs_b"], sb["tmp"][:, 0:256], sb["bcol"][:, 0:1], None,
                              OP.mult, deps=[r2, eb])
                else:
                    ba = c.tt(sb["bacol"], sb["bcol"], sb["acol"], OP.mult, deps=[eb, a1])
                    r3 = c.ts(sb["rhs_b"], c.PS[0][:, 0:256], sb["bacol"][:, 0:1], None,
                              OP.mult, deps=[tkz, ba])
                    c.consume(0, r3)
                tu = c.mm(2, sb["tmT"], sb["rhs_b"], True, True, deps=[r3, ltm], cols=256)
                eu = c.cp(sb["ub"], c.PS[2][:, 0:256], deps=[tu])
                c.consume(2, eu)
                trd = c.mm(3, sb["wT"], sb["ub"], True, True, deps=[eu, lwt], cols=256)
                eqz = c.cp(sb["tmpg"][:, 0:256], c.PS[1][:, 0:256],
                           deps=[tqz] + carry.get("tmpgr", []), eng="gpsimd")
                c.consume(1, eqz)
                sl_r = (slice(ci * DC, (ci + 1) * DC), slice(h * 256, (h + 1) * 256))
                if not correction:
                    rd = c.stt(sb["reads"], sb["tmpg"][:, 0:256], sb["acol"][:, 0:1],
                               c.PS[3][:, 0:256], OP.mult, OP.add,
                               deps=[eqz, trd, a1] + carry.get("readsw", []))
                    c.consume(3, rd)
                    st = c.store(c.ap["reads_sp"][sl_r[0], sl_r[1]], sb["reads"],
                                 deps=[rd])
                    carry["readsw"] = [st]
                    carry["tmpgr"] = [rd]
                    carry["acolr"] = [rd]
                else:
                    lr = c.load(sb["reads"], c.ap["reads_sp"][sl_r[0], sl_r[1]],
                                deps=carry.get("readsw", []), q="sync")
                    rd1 = c.stt(sb["tmpg"][:, 0:256], sb["tmpg"][:, 0:256],
                                sb["acol"][:, 0:1], c.PS[3][:, 0:256], OP.mult,
                                OP.subtract, deps=[eqz, trd, a1], eng="gpsimd")
                    c.consume(3, rd1)
                    rd = c.tt(sb["reads"], sb["reads"], sb["tmpg"][:, 0:256], OP.add,
                              deps=[rd1, lr])
                    carry["tmpgr"] = [rd]
                    carry["acolr"] = [rd1]
                tzs = []
                for kc2 in range(2):
                    tz = c.mm(4, sb["ktdc"][:, kc2 * 128:(kc2 + 1) * 128], sb["ub"],
                              True, True, deps=[eu, lkt], cols=256)
                    tzs.append(tz)
                    zt = c.stt(Z[kc2], Z[kc2], sb["accol"][:, 0:1], c.PS[4][:, 0:256],
                               OP.mult, OP.add if not correction else OP.subtract,
                               deps=[tz, a2, zbf[kc2]])
                    c.consume(4, zt)
                    carry[("z", h, kc2)] = [zt]
                    carry["accolr"].append(zt)
                carry["ld"] = [tkz, tqz, tu, trd] + tzs +                     ([tv0, tv1] if not correction else [])
                if correction:
                    sq1 = c.act(sb["tmpb"][:, 0:256], sb["reads"], AF.Square,
                                accum=sb["arow"], deps=[rd] + carry.get("arowr", []))
                    rsa = c.act(sb["arow"], sb["arow"], AF.Sqrt, scale=1.0 / 256.0,
                                bias=1e-6, deps=[sq1])
                    rs = c.recip(sb["arow"], sb["arow"], deps=[rsa])
                    rn = c.ts(sb["readb"], sb["reads"], sb["arow"][:, 0:1], None,
                              OP.mult, deps=[rs] + carry.get("readbr", []))
                    carry["readsw"] = [rn]
                    rts = []
                    for fs in range(2):
                        tt_ = c.mm(5, sb["readb"][:, fs * 128:(fs + 1) * 128],
                                   sb["ident_b"], True, True, deps=[rn], cols=128)
                        e = c.cp(sb["rfm"][2 * h + fs][:, (ci % 4) * 128:(ci % 4 + 1) * 128],
                                 c.PS[5][:, 0:128], deps=[tt_])
                        c.consume(5, e)
                        rfm_t[(2 * h + fs, ci % 4)] = e
                        rts.append(tt_)
                    carry["readbr"] = rts
                    carry["arowr"] = [rn]
            p.soft_barrier("l1sw" + ("c" if correction else "l"), lag=8)
            if correction and ci % 4 == 3:
                ng = ci // 4
                slg = slice(ng * CH, (ng + 1) * CH)
                ogl = []
                first = carry.get("oggr", [])
                for kc in range(KC):
                    ogl.append(c.load(sb["ogg"][kc],
                                      c.ap["og_sp"][kc * 128:(kc + 1) * 128, slg],
                                      deps=first, q="gpsimd"))
                    first = []
                ogl = [ogl[-1]] * KC
                ldf, ldb, tns = pre_ln(c, ng, "x_res", deps=[])
                ytick, ylist = [], []
                oggr = []
                for es_ in range(8):
                    bank = (0, 1)[es_ % 2]
                    t = None
                    for kc in range(KC):
                        t = c.mm(bank, sb["Wout"][kc][:, es_ * 128:(es_ + 1) * 128],
                                 sb["rfm"][kc], kc == 0, kc == KC - 1,
                                 deps=[rfm_t[(kc, 0)], rfm_t[(kc, 1)], rfm_t[(kc, 2)],
                                       rfm_t[(kc, 3)]])
                    e = c.ts(sb["ao"][es_], c.PS[bank][:, :],
                             sb["bout_c"][es_][:, 0:1], sb["rsig"][es_][:, 0:1],
                             OP.add, OP.mult, deps=[t])
                    c.consume(bank, e)
                    g2 = c.tt(sb["ao"][es_], sb["ao"][es_], sb["ogg"][es_], OP.mult,
                              deps=[e, ogl[es_]])
                    oggr.append(g2)
                    ytick.append(g2)
                    ylist.append(sb["ao"][es_])
                carry["oggr"] = oggr
                post_ln_residual(c, ng, ylist, ytick, "x_res")
        p.barrier()



def rowmax_fm(c, mxt):
    sb = c.sb
    tb = c.cp(sb["tmpb"], sb["tmp"], deps=[mxt])
    cols = []
    for j in range(4):
        tj = c.mm(6, sb["tmpb"][:, j * 128:(j + 1) * 128], sb["ident_b"], True, True,
                  deps=[tb], cols=128)
        e = c.p.emit("vector", "reduce_max", deps=[tj], out=sb["m128"][:, j:j + 1],
                     in_=c.PS[6][:, 0:128], axis=mybir.AxisListType.X)
        c.consume(6, e)
        cols.append(e)
    cb = c.cp(sb["tmpb"][:, 0:4], sb["m128"][:, 0:4], deps=cols)
    ts_ = []
    for j in range(4):
        ts_.append(c.mm(6, sb["tmpb"][:, j:j + 1], sb["ident_b"], True, True,
                        deps=[cb], rows=1, cols=128, coff=j * 128))
    er = c.cp(sb["row"], c.PS[6][0:1, 0:CH], deps=ts_)
    c.consume(6, er)
    return er


def l1_ffn(c):
    sb, p = c.sb, c.p

    def wquant(src_name, dst, nk, psb, sd):
        ncols = 4096 if nk == KC else 1024
        ngrp = ncols // 512
        s_t = None
        prev = []
        for kc in range(nk):
            l = c.load(sb["wtile"][:, 0:ncols],
                       c.ap[src_name][kc * 128:(kc + 1) * 128, :], q="sync", deps=prev)
            for j in range(ngrp):
                ta = c.act(sb["sq"], sb["wtile"][:, j * 512:(j + 1) * 512], AF.Abs,
                           deps=[l] + ([s_t] if s_t else []))
                s_t = c.mm(psb, sb["ones_col_b"], sb["sq"], (kc == 0 and j == 0),
                           (kc == nk - 1 and j == ngrp - 1), deps=[ta], rows=1, cols=512)
            prev = [l, s_t]
        e1 = c.cp(sb["row"], c.PS[psb][0:1, :], deps=[s_t])
        c.consume(psb, e1)
        r1 = c.redsum(sb["row2"][0:1, 0:1], sb["row"][0:1, :], deps=[e1])
        r2 = c.ts(sb["row2"][0:1, 0:1], sb["row2"][0:1, 0:1], 1.0 / (1024.0 * 4096.0),
                  None, OP.mult, deps=[r1])
        r3 = c.ts(sb["row2"][0:1, 0:1], sb["row2"][0:1, 0:1], 1e-5, None, OP.max,
                  deps=[r2])
        td = c.mm(psb, sb["ones_row_f"][0:1, :], sb["row2"][0:1, 0:1], True, True,
                  deps=[r3], cols=1)
        ed = c.cp(sb[sd], c.PS[psb][:, 0:1], deps=[td])
        r4 = c.recip(sb["row2"][0:1, 0:1], sb["row2"][0:1, 0:1], deps=[r3, td])
        tq = c.mm(psb, sb["ones_row_f"][0:1, :], sb["row2"][0:1, 0:1], True, True,
                  deps=[r4, ed], cols=1, coff=1)
        es1 = c.cp(sb["sqc"], c.PS[psb][:, 1:2], deps=[tq])
        c.consume(psb, es1)
        prev = [es1]
        last = None
        for kc in range(nk):
            l = c.load(sb["wtile"][:, 0:ncols],
                       c.ap[src_name][kc * 128:(kc + 1) * 128, :], q="sync", deps=prev)
            q1 = c.ts(sb["wtile"][:, 0:ncols], sb["wtile"][:, 0:ncols],
                      sb["sqc"][:, 0:1], None, OP.mult, deps=[l, es1])
            for j in range(ngrp):
                i1 = c.cp(sb["ii32"], sb["wtile"][:, j * 512:(j + 1) * 512], deps=[q1])
                i2 = c.cp(sb["tmp"], sb["ii32"], deps=[i1])
                last = c.ts(dst[kc][:, j * 512:(j + 1) * 512], sb["tmp"], -1.0, 1.0,
                            OP.max, OP.min, deps=[i2])
            prev = [last]
        return last

    hA = []
    hedge_t = []
    with c.phase() as es:
        c.alloc(es, "w1q", (KC, 128, 4096), BF16)
        c.alloc(es, "xiq", (KC, 128, CH), BF16)
        c.alloc(es, "wtile", (128, 4096), F32)
        WQ1 = wquant("l1f_w1", sb["w1q"], KC, 6, "s1d")
        p.barrier()
        for n in range(NCH):
            sl = slice(n * CH, (n + 1) * CH)
            ldf, ldb, tns = pre_ln(c, n, "x_res", deps=[], want_f32=False)
            mx = None
            for kc in range(KC):
                a = c.act(sb["sq"], sb["xn"][kc], AF.Abs,
                          deps=[tns[kc]] + ([mx] if mx else []))
                if kc == 0:
                    mx = c.cp(sb["tmp"], sb["sq"], deps=[a])
                else:
                    mx = c.tt(sb["tmp"], sb["tmp"], sb["sq"], OP.max, deps=[a, mx])
            er = rowmax_fm(c, mx)
            m1 = c.ts(sb["row"], sb["row"], 1e-5, None, OP.max, deps=[er])
            m2 = c.recip(sb["row2"], sb["row"], deps=[m1])
            m3 = c.ts(sb["row"], sb["row2"], 127.0, None, OP.mult, deps=[m2])
            tb1 = c.bcast(6, sb["row"], [m3])
            e1 = c.cp(sb["bc1"], c.PS[6][:, :], deps=[tb1])
            c.consume(6, e1)
            m4 = c.recip(sb["row2"], sb["row"], deps=[m3])
            tb2 = c.bcast(7, sb["row2"], [m4])
            e2 = c.cp(sb["bc2"], c.PS[7][:, :], deps=[tb2])
            c.consume(7, e2)
            xit = []
            for kc in range(KC):
                y1 = c.tt(sb["tmp"], sb["xn"][kc], sb["bc1"], OP.mult,
                          deps=[e1, tns[kc]])
                i1 = c.cp(sb["ii32"], sb["tmp"], deps=[y1])
                i2 = c.cp(sb["tmp"], sb["ii32"], deps=[i1])
                xit.append(c.ts(sb["xiq"][kc], sb["tmp"], -128.0, 127.0, OP.max, OP.min,
                                deps=[i2]))
            prev_st = []
            for hs in range(32):
                bank = (0, 1, 3, 5)[hs % 4]
                t = None
                for kc in range(KC):
                    t = c.mm(bank, sb["w1q"][kc][:, hs * 128:(hs + 1) * 128],
                             sb["xiq"][kc], kc == 0, kc == KC - 1, deps=[xit[kc], WQ1])
                d1 = c.tt(sb["tmpg"], c.PS[bank][:, :], sb["bc2"], OP.mult,
                          deps=[t, e2] + prev_st, eng="gpsimd")
                c.consume(bank, d1)
                d2 = c.ts(sb["tmpb"], sb["tmpg"], sb["s1d"][:, 0:1], None, OP.mult,
                          deps=[d1], eng="gpsimd")
                st = c.store(c.ap["h_sp"][hs * 128:(hs + 1) * 128, sl], sb["tmpb"],
                             deps=[d2])
                prev_st = [st]
                hA.append(st)
                if n == 0:
                    hedge_t.append(c.cp(sb["hedge"][hs][:, 0:1], sb["tmpb"][:, 0:1],
                                        deps=[d2], eng="gpsimd"))
                if n == NCH - 1:
                    hedge_t.append(c.cp(sb["hedge"][hs][:, 1:2],
                                        sb["tmpb"][:, CH - 1:CH],
                                        deps=[d2], eng="gpsimd"))
            p.soft_barrier("l1fa")
        p.barrier()
    HA = hA[-1]

    sc_ = []
    for hs in range(32):
        sc_.append(c.store(c.ap["sndC"][hs * 128:(hs + 1) * 128, :], sb["hedge"][hs],
                           deps=hedge_t))
    g = c.gather(c.ap["sndC"], c.ap["rcvC"], deps=sc_)
    hhl = []
    first = [g]
    for hs in range(32):
        l1 = c.load(sb["hhal"][hs][:, 0:1],
                    c.ap["rcvC"][0][hs * 128:(hs + 1) * 128, 1:2], deps=first, q="gpsimd")
        first = []
        l2 = c.load(sb["hhal"][hs][:, 1:2],
                    c.ap["rcvC"][1][hs * 128:(hs + 1) * 128, 0:1], deps=[], q="gpsimd")
        hhl.append((l2, l2))

    with c.phase() as es:
        c.alloc(es, "w2q", (32, 128, 1024), BF16)
        with c.phase() as es2:
            c.alloc(es2, "wtile", (128, 4096), F32)
            WQ2 = wquant("l1f_w2", sb["w2q"], 32, 7, "s2d")
            p.barrier()
        c.alloc(es, "hq", (32, 128, CH), BF16)
        c.alloc(es, "hwin", (128, CH + 2), BF16)
        for n in range(NCH):
            sl = slice(n * CH, (n + 1) * CH)
            ldf, ldb, tns = pre_ln(c, n, "x_res", deps=[])
            lo = max(0, n * CH - 1)
            hi = min(TC, (n + 1) * CH + 1)
            off = 1 if n == 0 else 0
            convt = []
            for hs in range(32):
                deps_h = [HA] + (convt[-1:] if convt else [])
                l = c.load(sb["hwin"][:, off:off + (hi - lo)],
                           c.ap["h_sp"][hs * 128:(hs + 1) * 128, lo:hi], deps=deps_h,
                           q="sync")
                if n == 0:
                    l = c.ts(sb["hwin"][:, 0:1], sb["hhal"][hs][:, 0:1],
                             sb["is_sec_c"][:, 0:1], None, OP.mult,
                             deps=[l, hhl[hs][0]], eng="gpsimd")
                if n == NCH - 1:
                    l = c.ts(sb["hwin"][:, CH + 1:CH + 2], sb["hhal"][hs][:, 1:2],
                             sb["is_fst_c"][:, 0:1], None, OP.mult,
                             deps=[l, hhl[hs][1]], eng="gpsimd")

                def wcol(tau, hs=hs):
                    return sb["l1f_cw_c"][hs][:, tau:tau + 1]
                tcv = c.ts(sb["tmp"], sb["hwin"][:, 0:CH], wcol(0), None, OP.mult,
                           deps=[l])
                for tau in range(1, 3):
                    tcv = c.stt(sb["tmp"], sb["hwin"][:, tau:tau + CH], wcol(tau),
                                sb["tmp"], OP.mult, OP.add, deps=[tcv])
                convt.append(c.act(sb["hq"][hs], sb["tmp"], AF.Silu,
                                   bias=sb["l1f_cb_c"][hs][:, 0:1], deps=[tcv]))
            mx = None
            for hs in range(32):
                a = c.act(sb["sq"], sb["hq"][hs], AF.Abs,
                          deps=[convt[hs]] + ([mx] if mx else []))
                if hs == 0:
                    mx = c.cp(sb["tmp"], sb["sq"], deps=[a])
                else:
                    mx = c.tt(sb["tmp"], sb["tmp"], sb["sq"], OP.max, deps=[a, mx])
            er = rowmax_fm(c, mx)
            m1 = c.ts(sb["row"], sb["row"], 1e-5, None, OP.max, deps=[er])
            m2 = c.recip(sb["row2"], sb["row"], deps=[m1])
            m3 = c.ts(sb["row"], sb["row2"], 127.0, None, OP.mult, deps=[m2])
            tb1 = c.bcast(6, sb["row"], [m3])
            e1 = c.cp(sb["bc1"], c.PS[6][:, :], deps=[tb1])
            c.consume(6, e1)
            m4 = c.recip(sb["row2"], sb["row"], deps=[m3])
            tb2 = c.bcast(7, sb["row2"], [m4])
            e2 = c.cp(sb["bc2"], c.PS[7][:, :], deps=[tb2])
            c.consume(7, e2)
            hqt = []
            for hs in range(32):
                y1 = c.tt(sb["tmp"], sb["hq"][hs], sb["bc1"], OP.mult,
                          deps=[e1, convt[hs]])
                i1 = c.cp(sb["ii32"], sb["tmp"], deps=[y1])
                i2 = c.cp(sb["tmp"], sb["ii32"], deps=[i1])
                hqt.append(c.ts(sb["hq"][hs], sb["tmp"], -128.0, 127.0, OP.max, OP.min,
                                deps=[i2]))
            ytick, ylist = [], []
            for es_ in range(8):
                bank = (0, 1, 3, 5)[es_ % 4]
                t = None
                for kc in range(32):
                    t = c.mm(bank, sb["w2q"][kc][:, es_ * 128:(es_ + 1) * 128],
                             sb["hq"][kc], kc == 0, kc == 31, deps=[hqt[kc], WQ2])
                d1 = c.tt(sb["tmpg"], c.PS[bank][:, :], sb["bc2"], OP.mult,
                          deps=[t, e2], eng="gpsimd")
                c.consume(bank, d1)
                d2 = c.ts(sb["ao"][es_], sb["tmpg"], sb["s2d"][:, 0:1], None, OP.mult,
                          deps=[d1], eng="gpsimd")
                ytick.append(d2)
                ylist.append(sb["ao"][es_])
            post_ln_residual(c, n, ylist, ytick, "out")
            p.soft_barrier("l1fb")
        p.barrier()


def emit_all(c):
    sb, p = c.sb, c.p
    cst = []
    cst.append(c.load(sb["cvals_sb"][:], c.ap["cvals"]))
    pairs = [("ident_f", "ident"), ("ident2_f", "ident2"), ("maddT_f", "maddT"),
             ("madl_f", "madd_low"), ("ones_row_f", "ones_row"),
             ("is_sec_c", "is_sec"), ("is_fst_c", "is_fst"),
             ("one_f", "one"), ("ident_b", "ident"), ("ones_col_b", "ones_col"),
             ("bab_c", "l1a_bab")]
    for dst, src in pairs:
        cst.append(c.load(sb[dst][:], c.ap[src]))
    for i in range(4):
        cst.append(c.load(sb["gnw_c"][i], c.ap["l0a_gnw"][i * 128:(i + 1) * 128, :]))
    for i in range(8):
        cst.append(c.load(sb["l0f_wb_c"][i], c.ap["l0f_wb"][i * 128:(i + 1) * 128, :]))
        cst.append(c.load(sb["bog_c"][i], c.ap["l1a_bog"][i * 128:(i + 1) * 128, :]))
        cst.append(c.load(sb["bout_c"][i], c.ap["l1a_bout"][i * 128:(i + 1) * 128, :]))
        cst.append(c.load(sb["ramp_c"][i], c.ap["l1a_ramp"][i * 128:(i + 1) * 128, :]))
    for gi in range(3):
        for j in range(8):
            cst.append(c.load(sb["cw3"][gi][j],
                              c.ap["l1a_cw"][gi][j * 128:(j + 1) * 128, :]))
    for i in range(32):
        cst.append(c.load(sb["l1f_cw_c"][i], c.ap["l1f_cw"][i * 128:(i + 1) * 128, :]))
        cst.append(c.load(sb["l1f_cb_c"][i], c.ap["l1f_cb"][i * 128:(i + 1) * 128, :]))
    CST = cst[-1]

    if "l0a" in STAGES:
        l0_attn(c, CST)
        p.barrier()
    if "l0f" in STAGES:
        l0_ffn(c)
        p.barrier()
    if "l1a" in STAGES:
        l1_attn(c)
        p.barrier()
    if "l1f" in STAGES:
        l1_ffn(c)
    p.barrier()


def kernel(**inp):
    xs = np.asarray(inp["L0_attn_x"])
    nc = build_program()
    shared = {}
    shared["l0a_wq"] = np.ascontiguousarray(np.asarray(inp["L0_attn_Wq"]).T)
    shared["l0a_wk"] = np.ascontiguousarray(np.asarray(inp["L0_attn_Wk"]).T)
    shared["l0a_wv"] = np.ascontiguousarray(np.asarray(inp["L0_attn_Wv"]).T)
    shared["l0a_wg"] = np.ascontiguousarray(np.asarray(inp["L0_attn_Wg"]).T)
    shared["l0a_wo"] = np.ascontiguousarray(np.asarray(inp["L0_attn_Wo"]).T)
    shared["l0a_gnw"] = np.ascontiguousarray(np.asarray(inp["L0_attn_gnorm_w"])[:, None])
    shared["l0f_wq"] = np.ascontiguousarray(np.asarray(inp["L0_ffn_Wq"]).T)
    rt = np.zeros((1024, 256), np.float32)
    rt[:, 0:255] = np.asarray(inp["L0_ffn_routers"]).T
    shared["l0f_rt"] = rt
    shared["l0f_lv"] = np.ascontiguousarray(np.asarray(inp["L0_ffn_leaf_values"]))
    shared["l0f_wo"] = np.ascontiguousarray(np.asarray(inp["L0_ffn_Wout_w"]).T)
    shared["l0f_wb"] = np.ascontiguousarray(np.asarray(inp["L0_ffn_Wout_b"])[:, None])
    shared["l1a_win"] = np.ascontiguousarray(np.asarray(inp["L1_attn_W_in"]))
    wctrl = np.asarray(inp["L1_attn_W_ctrl"])
    shared["l1a_wab"] = np.ascontiguousarray(wctrl[:, 0:8])
    shared["l1a_wog"] = np.ascontiguousarray(wctrl[:, 8:])
    bctrl = np.asarray(inp["L1_attn_b_ctrl"])
    shared["l1a_bab"] = np.ascontiguousarray(bctrl[0:8][:, None])
    shared["l1a_bog"] = np.ascontiguousarray(bctrl[8:][:, None])
    shared["l1a_cw"] = np.ascontiguousarray(np.stack(
        [np.asarray(inp["L1_attn_w_q"])[:, 0, :],
         np.asarray(inp["L1_attn_w_k"])[:, 0, :],
         np.asarray(inp["L1_attn_w_v"])[:, 0, :]], 0).astype(np.float32))
    shared["l1a_wout"] = np.ascontiguousarray(np.asarray(inp["L1_attn_W_out"]))
    shared["l1a_bout"] = np.ascontiguousarray(np.asarray(inp["L1_attn_b_out"])[:, None])
    shared["l1a_ramp"] = np.ascontiguousarray(np.asarray(inp["L1_attn_ramp_logit"])[:, None])
    shared["l1f_w1"] = np.ascontiguousarray(np.asarray(inp["L1_ffn_w1"]).T)
    shared["l1f_cw"] = np.ascontiguousarray(np.asarray(inp["L1_ffn_conv_w"])[:, 0, :])
    shared["l1f_cb"] = np.ascontiguousarray(np.asarray(inp["L1_ffn_conv_b"])[:, None])
    shared["l1f_w2"] = np.ascontiguousarray(np.asarray(inp["L1_ffn_w2"]).T)
    in_maps = []
    for b in range(4):
        for s in range(2):
            m = dict(shared)
            m["x_in"] = np.ascontiguousarray(
                xs[b, s * TC:(s + 1) * TC, :].T.astype(np.float32))
            m.update(host_tables(s))
            m["is_sec"] = np.full((128, 1), float(s), np.float32)
            m["cvals"] = np.broadcast_to(
                np.array([1e-5, 1e-6, 1.0 / 256.0], np.float32)[None, :],
                (128, 3)).copy()
            m["is_fst"] = np.full((128, 1), float(1 - s), np.float32)
            in_maps.append(m)
    res = run_bass_kernel_spmd(nc, in_maps, core_ids=list(range(8)))
    out = np.zeros((4, 8192, 1024), np.float32)
    for b in range(4):
        for s in range(2):
            out[b, s * TC:(s + 1) * TC, :] = res.results[2 * b + s]["out"].T
    return out


# version 13
# speedup vs baseline: 1.0511x; 1.0511x over previous
import numpy as np
import concourse.bass as bass
import concourse.mybir as mybir
from concourse.bass_utils import run_bass_kernel_spmd

F32 = mybir.dt.float32
BF16 = mybir.dt.bfloat16
I32 = mybir.dt.int32
AF = mybir.ActivationFunctionType
OP = mybir.AluOpType

TC = 4096
CH = 512
DC = 128
D = 1024
KC = D // 128
NCH = TC // CH
NDC = TC // DC
NEG = -30000.0
RG = [[0, 1], [2, 3], [4, 5], [6, 7]]
GAM = [float(1.0 - 2.0 ** (-5.0 - h)) for h in range(4)]
STAGES = ("l0a", "l0f", "l1a", "l1f")

ENGINES = ("tensor", "vector", "scalar", "gpsimd", "sync")


class Prog:
    def __init__(self, nc, sems):
        self.nc = nc
        self.sems = sems
        self.count = {k: 0 for k in sems}
        self.alias = {}
        self.qidx = {}
        self.last_item = {}
        self.g_read = {}
        self.g_write = {}
        self.runends = {}
        self.stream = {e: [] for e in ENGINES}
        self.water = {e: {k: 0 for k in sems} for e in ENGINES}

    def emit(self, eng, fn, *args, deps=(), inc=None, amt=None, **kw):
        is_dma = fn == "dma_start"
        rotated = False
        if inc is None:
            if is_dma:
                qk = "q_" + eng
                has_deps = any(d is not None and d[1] > 0 for d in deps)
                if self.last_item.get(eng) != qk or has_deps:
                    old_sem = f"{qk}{self.qidx[qk]}" if qk in self.qidx else None
                    if old_sem is not None:
                        self.runends.setdefault(old_sem, []).append(self.count[old_sem])
                    self.qidx[qk] = (self.qidx.get(qk, -1) + 1) % 16
                    rotated = True
                inc = f"{qk}{self.qidx[qk]}"
            else:
                inc = eng
        self.last_item[eng] = ("q_" + eng) if is_dma else None
        if amt is None:
            amt = 16 if is_dma else 1
        def tname(x):
            t = getattr(x, "tensor", None)
            if t is not None:
                n = getattr(t, "name", None)
            else:
                n = getattr(x, "name", None) if "TensorHandle" in type(x).__name__ else None
            if n is None or n.startswith("ps"):
                return None
            return n
        wr_keys = ("out", "out_ap", "accum_out")
        reads, writes = [], []
        if fn == "memset":
            n = tname(args[0]) if args else None
            if n:
                writes.append(n)
        else:
            for k, v in kw.items():
                n = tname(v)
                if n:
                    (writes if k in wr_keys else reads).append(n)
            for a in args:
                n = tname(a)
                if n:
                    reads.append(n)
        deps = list(deps)
        if is_dma and rotated and self.count[inc] > 0:
            deps.append((inc, self.count[inc]))
        for n in writes:
            deps += self.g_read.get(n, [])
            for d in self.g_write.get(n, []):
                if not (is_dma and d[0].startswith("q_")):
                    deps.append(d)
        for n in reads:
            deps += self.g_write.get(n, [])
        real = []
        if (not is_dma) and eng in ("gpsimd", "vector", "scalar"):
            deps = deps + [(eng, self.count[eng])]
        for d in deps:
            if d is None:
                continue
            s, v = d
            if v > 0 and self.water[eng][s] < v:
                real.append((s, v))
                self.water[eng][s] = v
        self.count[inc] += amt
        tick = (inc, self.count[inc])
        self.stream[eng].append((real, fn, args, kw, tick, amt))
        for n in writes:
            self.g_write[n] = [tick]
            self.g_read[n] = []
        for n in reads:
            self.g_read.setdefault(n, []).append(tick)
        return tick

    def barrier(self):
        for qk, idx in self.qidx.items():
            cur = f"{qk}{idx}"
            if self.count.get(cur, 0) > 0:
                self.runends.setdefault(cur, []).append(self.count[cur])
        self.last_item = {}
        snap = [(k, v) for k, v in self.count.items() if v > 0]
        for e in ENGINES:
            self.count[e] += 1
            self.stream[e].append((
                [(k, v) for (k, v) in snap if self.water[e][k] < v],
                "__nop__", (), {}, (e, self.count[e]), 1))
            for k, v in snap:
                if self.water[e][k] < v:
                    self.water[e][k] = v

    def finalize(self, blk):
        milestones = {}
        for k, v in self.runends.items():
            milestones[k] = list(v)
        for k in self.count:
            if k.startswith("q_") and self.count[k] > 0:
                milestones.setdefault(k, []).append(self.count[k])
        for k in milestones:
            milestones[k] = sorted(set(milestones[k]))

        import bisect

        def roundup(sem, v):
            ms = milestones.get(sem)
            if not ms:
                return v
            i = bisect.bisect_left(ms, v)
            return ms[i] if i < len(ms) else ms[-1]

        for eng in ENGINES:
            for item in self.stream[eng]:
                deps = item[0]
                for i, (sm, v) in enumerate(deps):
                    if sm.startswith("q_"):
                        deps[i] = (sm, roundup(sm, v))
        for eng in ENGINES:
            ins = self.stream[eng]
            if not ins:
                continue

            def fix(a):
                return a[:] if "TensorHandle" in type(a).__name__ else a

            def body(h, ins=ins):
                for (deps, fn, args, kw, tick, amt) in ins:
                    for (s, v) in deps:
                        h.wait_ge(self.sems[s], v)
                    if fn == "__nop__":
                        h.nop().then_inc(self.sems[tick[0]], amt)
                    else:
                        args = [fix(a) for a in args]
                        kw = {k: fix(v) for k, v in kw.items()}
                        getattr(h, fn)(*args, **kw).then_inc(self.sems[tick[0]], amt)
            getattr(blk, eng)(body)


class Ctx:
    def __init__(self, p, ap, sb, PS, nc):
        self.p, self.ap, self.sb, self.PS = p, ap, sb, PS
        self.nc = nc
        self.ps_owner = [("tensor", 0)] * 8
        self._phase_id = 0

    def phase(self):
        import contextlib
        self._phase_id += 1
        return contextlib.ExitStack()

    def alloc(self, es, name, shape, dt):
        nm = f"ph{self._phase_id}_{name}"

        def rec(nm, shp):
            if len(shp) == 2:
                return es.enter_context(self.nc.sbuf_tensor(nm, list(shp), dt))
            return [rec(f"{nm}_{i}", shp[1:]) for i in range(shp[0])]
        t = rec(nm, list(shape))
        self.sb[name] = t
        return t

    @staticmethod
    def _dt(x):
        t = getattr(x, "tensor", None)
        return t.dtype if t is not None else x.dtype

    def load(self, dst, src, deps=(), q="sync"):
        if self._dt(dst) != self._dt(src):
            q = "gpsimd"
        return self.p.emit(q, "dma_start", deps=deps, out=dst, in_=src,
                           allow_slow_non_contiguous=True)

    def store(self, dst, src, deps=(), q="gpsimd"):
        if self._dt(dst) != self._dt(src):
            q = "gpsimd"
        return self.p.emit(q, "dma_start", deps=deps, out=dst, in_=src,
                           allow_slow_non_contiguous=True)

    def mm(self, bank, lhsT, rhs, start, stop, deps=(), rows=128, roff=0, cols=512, coff=0):
        v = self.PS[bank][roff:roff + rows, coff:coff + cols]
        d = list(deps)
        if start:
            d.append(self.ps_owner[bank])
        return self.p.emit("tensor", "matmul", deps=d, out=v, lhsT=lhsT, rhs=rhs,
                           start=start, stop=stop)

    def consume(self, bank, tick):
        self.ps_owner[bank] = tick

    @staticmethod
    def _psafe(eng, *ops):
        if eng != "gpsimd":
            return eng
        for o in ops:
            sp = getattr(o, "space", None)
            if sp is not None and "PSUM" in str(sp):
                return "vector"
        return eng

    def tt(self, out, a, b, op, deps=(), eng="vector"):
        eng = self._psafe(eng, out, a, b)
        return self.p.emit(eng, "tensor_tensor", deps=deps, out=out, in0=a, in1=b, op=op)

    def ts(self, out, a, s1, s2, op0, op1=None, deps=(), eng="vector"):
        eng = self._psafe(eng, out, a)
        if eng == "gpsimd" and not isinstance(s1, (int, float)):
            eng = "vector"
        if op1 is None:
            return self.p.emit(eng, "tensor_scalar", deps=deps, out=out, in0=a,
                               scalar1=s1, scalar2=None, op0=op0)
        return self.p.emit(eng, "tensor_scalar", deps=deps, out=out, in0=a,
                           scalar1=s1, scalar2=s2, op0=op0, op1=op1)

    def stt(self, out, in0, scalar, in1, op0, op1, deps=(), eng="vector"):
        eng = self._psafe(eng, out, in0, in1)
        if eng == "gpsimd" and not isinstance(scalar, (int, float)):
            eng = "vector"
        return self.p.emit(eng, "scalar_tensor_tensor", deps=deps, out=out, in0=in0,
                           scalar=scalar, in1=in1, op0=op0, op1=op1)

    def cp(self, out, a, deps=(), eng="vector"):
        eng = self._psafe(eng, out, a)
        return self.p.emit(eng, "tensor_copy", deps=deps, out=out, in_=a)

    def act(self, out, a, fn, bias=0.0, scale=1.0, accum=None, deps=()):
        kw = dict(out=out, in_=a, func=fn, bias=bias, scale=scale)
        if accum is not None:
            kw["accum_out"] = accum
        return self.p.emit("scalar", "activation", deps=deps, **kw)

    def scan(self, out, d0, init, op0, deps=(), eng="vector"):
        return self.p.emit(eng, "tensor_tensor_scan", deps=deps, out=out, data0=d0,
                           data1=d0, initial=float(init), op0=op0, op1=OP.bypass)

    def recip(self, out, a, deps=(), eng="vector"):
        return self.p.emit(eng, "reciprocal", deps=deps, out=out, in_=a)

    def redsum(self, out, a, deps=(), eng="vector"):
        return self.p.emit(eng, "reduce_sum", deps=deps, out=out, in_=a,
                           axis=mybir.AxisListType.X)

    def gather(self, snd, rcv, deps=()):
        return self.p.emit("gpsimd", "collective_compute", "AllGather", OP.bypass,
                           deps=deps, inc="cc", amt=1,
                           replica_groups=RG, ins=[snd.opt()], outs=[rcv.opt()])

    def bcast(self, bank, row, deps, cols=CH, coff=0):
        return self.mm(bank, self.sb["ones_row_f"][0:1, :], row, True, True,
                       deps=deps, cols=cols, coff=coff)

    def rowT(self, bank, row, deps, n=128, coff=0):
        return self.mm(bank, row, self.sb["one_f"][0:1, 0:1], True, True,
                       deps=deps, rows=n, cols=1, coff=coff)


def host_tables(shalf):
    t = {}
    HK = 256
    inv_freq = 1.0 / (10000.0 ** (np.arange(0, HK, 2, dtype=np.float64) / HK))
    pos = np.arange(shalf * TC, (shalf + 1) * TC, dtype=np.float64)
    fr = inv_freq[:, None] * pos[None, :]
    t["rope_cos"] = np.cos(fr).astype(np.float32)
    t["rope_sin"] = np.sin(fr).astype(np.float32)
    gam = 1.0 - np.exp2(-5.0 - np.arange(4, dtype=np.float64))
    i = np.arange(CH, dtype=np.float64)
    sc = 1.0 / 16.0
    mask = np.where(i[None, :] - i[:, None] >= 0.0,
                    gam[:, None, None] ** (i[None, :] - i[:, None]), 0.0) * sc
    t["ret_maskT"] = mask.astype(np.float32)
    t["ret_cross"] = np.broadcast_to((gam[:, None] ** (i + 1.0) * sc)[:, None, :],
                                     (4, 128, CH)).astype(np.float32).copy()
    t["ret_kdec"] = np.broadcast_to((gam[:, None] ** (CH - 1.0 - i))[:, None, :],
                                    (4, 128, CH)).astype(np.float32).copy()
    DEPTH, M = 8, 256
    P = np.zeros((512, M), np.float32)
    for m in range(M):
        node = 0
        for l in range(DEPTH):
            bit = (m >> (DEPTH - 1 - l)) & 1
            col = (2 ** l - 1) + node
            P[col if bit else col + 256, m] = 1.0
            node = node * 2 + bit
    t["treeP"] = P
    t["ident"] = np.eye(128, dtype=np.float32)
    t["ident2"] = (2.0 * np.eye(128)).astype(np.float32)
    jj = np.arange(DC)
    keepT = (jj[None, :] >= jj[:, None])
    t["maddT"] = np.where(keepT, 0.0, NEG).astype(np.float32)
    t["madd_low"] = np.where(keepT.T, 0.0, -NEG).astype(np.float32)
    t["ones_row"] = np.ones((1, 128), np.float32)
    t["ones_col"] = np.ones((128, 1), np.float32)
    t["one"] = np.ones((1, 1), np.float32)
    return t


def build_program():
    nc = bass.Bass("TRN2", target_bir_lowering=False, debug=False)
    ap = {}

    def din(name, shape, dt=F32):
        ap[name] = nc.dram_tensor(name, list(shape), dt, kind="ExternalInput").ap()

    din("x_in", (D, TC))
    din("l0a_wq", (D, 1024)); din("l0a_wk", (D, 1024))
    din("l0a_wv", (D, 2048)); din("l0a_wg", (D, 2048))
    din("l0a_wo", (2048, D)); din("l0a_gnw", (512, 1))
    din("l0f_wq", (D, D)); din("l0f_rt", (D, 256))
    din("l0f_lv", (256, D)); din("l0f_wo", (D, D)); din("l0f_wb", (D, 1))
    din("l1a_win", (D, 3072)); din("l1a_wab", (D, 8)); din("l1a_wog", (D, D))
    din("l1a_bab", (8, 1)); din("l1a_bog", (D, 1))
    din("l1a_cw", (3, D, 4))
    din("l1a_wout", (D, D)); din("l1a_bout", (D, 1)); din("l1a_ramp", (D, 1))
    din("l1f_w1", (D, 4096)); din("l1f_cw", (4096, 3)); din("l1f_cb", (4096, 1))
    din("l1f_w2", (4096, D))
    din("rope_cos", (128, TC)); din("rope_sin", (128, TC))
    din("ret_maskT", (4, CH, CH)); din("ret_cross", (4, 128, CH)); din("ret_kdec", (4, 128, CH))
    din("treeP", (512, 256))
    din("ident", (128, 128)); din("ident2", (128, 128))
    din("maddT", (128, 128)); din("madd_low", (128, 128))
    din("ones_row", (1, 128)); din("ones_col", (128, 1)); din("one", (1, 1))
    din("is_sec", (128, 1)); din("is_fst", (128, 1))
    din("cvals", (128, 3))

    ap["out"] = nc.dram_tensor("out", [D, TC], F32, kind="ExternalOutput").ap()

    def dsc(name, shape, dt):
        ap[name] = nc.dram_tensor(name, list(shape), dt).ap()

    dsc("x_res", (D, TC), F32)
    dsc("o_sp", (2048, TC), BF16)
    dsc("qt_sp", (D, TC), BF16)
    dsc("reads_sp", (TC, D), BF16)
    dsc("h_sp", (4096, TC), BF16)
    dsc("q_sp", (D, TC), BF16); dsc("k_sp", (D, TC), BF16); dsc("v_sp", (D, TC), BF16)
    dsc("og_sp", (D, TC), BF16)
    dsc("tm_sp", (NDC * 4 * 128, 128), BF16)
    dsc("wt_sp", (NDC * 4 * 128, 128), BF16)
    dsc("ktd_sp", (4 * NDC * 128, 256), BF16)
    dsc("ab_sp", (8, TC), F32); dsc("lg_sp", (4, TC), F32)
    dsc("sndA", (D, CH), BF16); dsc("rcvA", (2, D, CH), BF16)
    dsc("sndA2", (D, 3), F32); dsc("rcvA2", (2, D, 3), F32)
    dsc("sndB", (D, 256), BF16); dsc("rcvB", (2, D, 256), BF16)
    dsc("sndC", (4096, 2), F32); dsc("rcvC", (2, 4096, 2), F32)

    import contextlib
    es = contextlib.ExitStack()
    sb = {}

    def S(name, shape, dt):
        def rec(nm, shp):
            if len(shp) == 2:
                return es.enter_context(nc.sbuf_tensor(nm, list(shp), dt))
            return [rec(f"{nm}_{i}", shp[1:]) for i in range(shp[0])]
        sb[name] = rec(name, list(shape))

    with es:
        S("ident_f", (128, 128), F32); S("ident_b", (128, 128), BF16)
        S("ident2_f", (128, 128), F32)
        S("maddT_f", (128, 128), F32); S("madl_f", (128, 128), F32)
        S("ones_row_f", (1, 128), F32); S("ones_col_b", (128, 1), BF16)
        S("one_f", (1, 1), F32)
        S("is_sec_c", (128, 1), F32); S("is_fst_c", (128, 1), F32)
        S("gnw_c", (4, 128, 1), F32); S("l0f_wb_c", (8, 128, 1), F32)
        S("bab_c", (8, 1), F32); S("bog_c", (8, 128, 1), F32)
        S("bout_c", (8, 128, 1), F32); S("ramp_c", (8, 128, 1), F32)
        S("rsig", (8, 128, 1), F32)
        S("cw3", (3, 8, 128, 4), F32)
        S("l1f_cw_c", (32, 128, 3), F32); S("l1f_cb_c", (32, 128, 1), F32)
        S("xf", (KC, 128, CH), F32); S("xb", (KC, 128, CH), BF16)
        S("xn", (KC, 128, CH), BF16)
        S("row", (1, CH), F32); S("row2", (1, CH), F32); S("row3", (1, CH), F32)
        S("bc1", (128, CH), F32); S("bc2", (128, CH), F32)
        S("sq", (128, CH), BF16)
        S("ao", (KC, 128, CH), BF16)
        S("tmp", (128, CH), F32); S("tmpg", (128, CH), F32)
        S("tmpb", (128, CH), BF16); S("tmpb2", (128, CH), BF16)
        S("ii32", (128, CH), I32)
        S("Zst", (4, 2, 128, 256), F32); S("Zin", (4, 2, 128, 256), F32)
        S("Zbf", (2, 128, 256), BF16)
        S("lac", (128, 1), F32); S("lacb", (128, 1), F32)
        S("acol", (128, 1), F32); S("bcol", (128, 1), F32)
        S("bacol", (128, 1), F32); S("accol", (128, 1), F32); S("dccol", (128, 1), F32)
        S("arow", (128, 1), F32); S("lrow", (1, 128), F32)
        S("lgrow", (1, 128), F32); S("brow", (1, 128), F32)
        S("s1d", (128, 1), F32); S("s2d", (128, 1), F32); S("sqc", (128, 1), F32)
        S("abr_c", (8, CH), F32); S("lga_c", (4, CH), F32)
        S("gpt", (128, 128), BF16); S("gpu", (128, 128), BF16)
        S("mmt", (128, 128), BF16)
        S("nx", (128, 128), BF16); S("nxt", (128, 128), BF16)
        S("nx2", (128, 128), BF16); S("nxt2", (128, 128), BF16)
        S("zz", (128, 128), BF16)
        S("m128", (128, 128), F32); S("m128b", (128, 128), F32)
        S("vtmt", (128, 256), BF16); S("ub", (128, 256), BF16)
        S("rhs_b", (128, 256), BF16)
        S("reads", (128, 256), F32); S("readb", (128, 256), BF16)
        S("hedge", (32, 128, 2), BF16); S("hhal", (32, 128, 2), F32)
        S("cvals_sb", (128, 3), F32)

        PS = [es.enter_context(nc.psum_tensor(f"ps{i}", [128, 512], F32)) for i in range(8)]

        sems = {}
        for e in ENGINES:
            sems[e] = es.enter_context(nc.semaphore("s_" + e))
        for e in ("sync", "gpsimd"):
            for i in range(16):
                sems[f"q_{e}{i}"] = es.enter_context(nc.semaphore(f"q_{e}{i}"))
        sems["cc"] = es.enter_context(nc.semaphore("s_cc"))

        blk = es.enter_context(nc.Block())
        nc.const_aps.register(F32, 1e-5, sb["cvals_sb"][:, 0:1])
        nc.const_aps.register(F32, 1e-6, sb["cvals_sb"][:, 1:2])
        nc.const_aps.register(F32, 1.0 / 256.0, sb["cvals_sb"][:, 2:3])
        p = Prog(nc, sems)
        c = Ctx(p, ap, sb, PS, nc)
        emit_all(c)
        p.finalize(blk)
    return nc


def pre_ln(c, n, src, deps=(), want_f32=True):
    sb = c.sb
    sl = slice(n * CH, (n + 1) * CH)
    ldf, ldb = [], []
    first = list(deps)
    for kc in range(KC):
        hsl = c.ap[src][kc * 128:(kc + 1) * 128, sl]
        if want_f32:
            ldf.append(c.load(sb["xf"][kc], hsl, deps=first, q="gpsimd"))
            first = []
        ldb.append(c.load(sb["xb"][kc], hsl, deps=first, q="gpsimd"))
        first = []
    t_sum = t_sq = None
    for kc in range(KC):
        t_sum = c.mm(6, sb["ones_col_b"], sb["xb"][kc], kc == 0, kc == KC - 1,
                     deps=[ldb[kc]], rows=1, cols=CH)
    sq_dep = list(deps)
    for kc in range(KC):
        tq = c.act(sb["sq"], sb["xb"][kc], AF.Square, deps=[ldb[kc]] + sq_dep)
        t_sq = c.mm(7, sb["ones_col_b"], sb["sq"], kc == 0, kc == KC - 1,
                    deps=[tq], rows=1, cols=CH)
        sq_dep = [t_sq]
    row, row2, row3 = sb["row"], sb["row2"], sb["row3"]
    t1 = c.ts(row, c.PS[6][0:1, :], 1.0 / D, None, OP.mult, deps=[t_sum])
    c.consume(6, t1)
    t2 = c.ts(row2, c.PS[7][0:1, :], 1.0 / D, None, OP.mult, deps=[t_sq])
    c.consume(7, t2)
    t3 = c.tt(row3, row, row, OP.mult, deps=[t1])
    t4 = c.tt(row2, row2, row3, OP.subtract, deps=[t2, t3])
    t5a = c.act(row2, row2, AF.Sqrt, bias=1e-5, deps=[t4])
    t5 = c.recip(row2, row2, deps=[t5a])
    t6 = c.tt(row3, row, row2, OP.mult, deps=[t5])
    tb1 = c.bcast(6, row2, [t5])
    tb2 = c.bcast(7, row3, [t6])
    e1 = c.cp(sb["bc1"], c.PS[6][:, :], deps=[tb1])
    c.consume(6, e1)
    e2 = c.cp(sb["bc2"], c.PS[7][:, :], deps=[tb2])
    c.consume(7, e2)
    tns = []
    for kc in range(KC):
        eng = "vector" if kc % 2 == 0 else "gpsimd"
        tb = sb["tmp"] if eng == "vector" else sb["tmpg"]
        ta = c.tt(tb, sb["xb"][kc], sb["bc1"], OP.mult, deps=[e1, ldb[kc]], eng=eng)
        tn = c.tt(sb["xn"][kc], tb, sb["bc2"], OP.subtract, deps=[ta, e2], eng=eng)
        tns.append(tn)
    return ldf, ldb, tns


def post_ln_residual(c, n, ylist, y_deps, dst):
    sb = c.sb
    sl = slice(n * CH, (n + 1) * CH)
    t_sum = t_sq = None
    for kc in range(KC):
        t_sum = c.mm(6, sb["ones_col_b"], ylist[kc], kc == 0, kc == KC - 1,
                     deps=[y_deps[kc]], rows=1, cols=CH)
    sq_dep = []
    for kc in range(KC):
        tq = c.act(sb["sq"], ylist[kc], AF.Square, deps=[y_deps[kc]] + sq_dep)
        t_sq = c.mm(7, sb["ones_col_b"], sb["sq"], kc == 0, kc == KC - 1,
                    deps=[tq], rows=1, cols=CH)
        sq_dep = [t_sq]
    row, row2, row3 = sb["row"], sb["row2"], sb["row3"]
    t1 = c.ts(row, c.PS[6][0:1, :], 1.0 / D, None, OP.mult, deps=[t_sum])
    c.consume(6, t1)
    t2 = c.ts(row2, c.PS[7][0:1, :], 1.0 / D, None, OP.mult, deps=[t_sq])
    c.consume(7, t2)
    t3 = c.tt(row3, row, row, OP.mult, deps=[t1])
    t4 = c.tt(row2, row2, row3, OP.subtract, deps=[t2, t3])
    t5a = c.act(row2, row2, AF.Sqrt, bias=1e-5, deps=[t4])
    t5 = c.recip(row2, row2, deps=[t5a])
    t6 = c.tt(row3, row, row2, OP.mult, deps=[t5])
    tb1 = c.bcast(6, row2, [t5])
    tb2 = c.bcast(7, row3, [t6])
    e1 = c.cp(sb["bc1"], c.PS[6][:, :], deps=[tb1])
    c.consume(6, e1)
    e2 = c.cp(sb["bc2"], c.PS[7][:, :], deps=[tb2])
    c.consume(7, e2)
    outticks = []
    for kc in range(KC):
        eng = "vector" if kc % 2 == 0 else "gpsimd"
        tb = sb["tmp"] if eng == "vector" else sb["tmpg"]
        ta = c.tt(tb, ylist[kc], sb["bc1"], OP.mult, deps=[e1, y_deps[kc]], eng=eng)
        tc_ = c.tt(tb, tb, sb["bc2"], OP.subtract, deps=[ta, e2], eng=eng)
        td = c.tt(sb["xf"][kc], sb["xf"][kc], tb, OP.add, deps=[tc_], eng=eng)
        st = c.store(c.ap[dst][kc * 128:(kc + 1) * 128, sl], sb["xf"][kc], deps=[td])
        outticks.append(st)
    return outticks


def l0_attn(c, CST):
    sb, p = c.sb, c.p
    with c.phase() as es:
        c.alloc(es, "rope_cos_b", (128, TC), BF16)
        c.alloc(es, "rope_sin_b", (128, TC), BF16)
        c.alloc(es, "maskT", (4, 4, 128, CH), BF16)
        c.alloc(es, "crossb", (4, 128, CH), BF16)
        c.alloc(es, "kdecb", (4, 128, CH), BF16)
        c.alloc(es, "qraw", (2, 128, CH), BF16)
        c.alloc(es, "kraw", (2, 128, CH), BF16)
        c.alloc(es, "qf", (2, 128, CH), BF16)
        c.alloc(es, "kf", (2, 128, CH), BF16)
        c.alloc(es, "qt", (2, 128, CH), BF16)
        c.alloc(es, "ktd", (2, 128, CH), BF16)
        c.alloc(es, "ktm", (4, 128, 256), BF16)
        c.alloc(es, "vtm", (4, 128, CH), BF16)
        c.alloc(es, "aT", (4, 128, CH), BF16)
        c.alloc(es, "Sbf", (2, 128, CH), BF16)
        c.alloc(es, "oev", (2, 128, CH), BF16)
        c.alloc(es, "Sst", (4, 2, 128, CH), F32)
        c.alloc(es, "WqG", (2, KC, 128, 256), BF16)
        c.alloc(es, "WkG", (2, KC, 128, 256), BF16)
        c.alloc(es, "WvG", (1, KC, 128, 512), BF16)
        cl = []
        cl.append(c.load(sb["rope_cos_b"][:], c.ap["rope_cos"]))
        cl.append(c.load(sb["rope_sin_b"][:], c.ap["rope_sin"]))
        for h in range(4):
            for js in range(4):
                cl.append(c.load(sb["maskT"][h][js],
                                 c.ap["ret_maskT"][h][js * 128:(js + 1) * 128, :]))
            cl.append(c.load(sb["crossb"][h], c.ap["ret_cross"][h]))
            cl.append(c.load(sb["kdecb"][h], c.ap["ret_kdec"][h]))
        CL = cl[-1]
        for h in range(4):
            for dkc in range(2):
                p.emit("vector", "memset", sb["Sst"][h][dkc][:, :], 0.0, deps=[])
        p.barrier()

        wg_prev = {}
        for n in range(NCH):
            sl = slice(n * CH, (n + 1) * CH)
            ldf, ldb, tns = pre_ln(c, n, "x_in", deps=[CST, CL])
            for h in range(4):
                pr = h % 2
                wts = []
                first = wg_prev.get(pr, []) + wg_prev.get("v", [])
                for kc in range(KC):
                    wts.append(c.load(sb["WqG"][pr][kc],
                                      c.ap["l0a_wq"][kc * 128:(kc + 1) * 128,
                                                     h * 256:(h + 1) * 256],
                                      deps=first))
                    first = []
                    wts.append(c.load(sb["WkG"][pr][kc],
                                      c.ap["l0a_wk"][kc * 128:(kc + 1) * 128,
                                                     h * 256:(h + 1) * 256],
                                      deps=[]))
                    wts.append(c.load(sb["WvG"][0][kc],
                                      c.ap["l0a_wv"][kc * 128:(kc + 1) * 128,
                                                     h * 512:(h + 1) * 512],
                                      deps=[]))
                WT_ = wts[-1]
                qk = {}
                lastw = []
                for (nm, W, bank) in (("q", "WqG", 0), ("k", "WkG", 1)):
                    for fc in range(2):
                        t = None
                        for kc in range(KC):
                            t = c.mm(bank, sb[W][pr][kc][:, fc * 128:(fc + 1) * 128],
                                     sb["xn"][kc], kc == 0, kc == KC - 1,
                                     deps=[tns[kc], WT_])
                        lastw.append(t)
                        dst = sb["qraw" if nm == "q" else "kraw"][fc]
                        e = c.cp(dst, c.PS[bank][:, :], deps=[t],
                                 eng="vector" if nm == "q" else "gpsimd")
                        c.consume(bank, e)
                        qk[(nm, fc)] = e
                vt = []
                for ts_ in range(4):
                    t = None
                    for kc in range(KC):
                        t = c.mm(3, sb["xn"][kc][:, ts_ * 128:(ts_ + 1) * 128],
                                 sb["WvG"][0][kc], kc == 0, kc == KC - 1,
                                 deps=[tns[kc], WT_])
                    lastw.append(t)
                    e = c.cp(sb["vtm"][ts_], c.PS[3][:, :], deps=[t], eng="gpsimd")
                    c.consume(3, e)
                    vt.append(e)
                wg_prev[pr] = lastw
                wg_prev["v"] = lastw
                cosv = sb["rope_cos_b"][:, sl]
                sinv = sb["rope_sin_b"][:, sl]
                rop = {}
                for (nm, raw, dst) in (("q", "qraw", "qf"), ("k", "kraw", "kf")):
                    d0 = [qk[(nm, 0)], qk[(nm, 1)]]
                    eng = "vector" if nm == "q" else "gpsimd"
                    tb, tb2 = ((sb["tmpb"], sb["tmpb2"]) if eng == "vector"
                               else (sb["sq"], sb["tmpg"]))
                    a1 = c.tt(tb, sb[raw][0], cosv, OP.mult, deps=d0, eng=eng)
                    a2 = c.tt(tb2, sb[raw][1], sinv, OP.mult, deps=d0, eng=eng)
                    r0 = c.tt(sb[dst][0], tb, tb2, OP.subtract, deps=[a1, a2], eng=eng)
                    a3 = c.tt(tb, sb[raw][1], cosv, OP.mult, deps=[r0], eng=eng)
                    a4 = c.tt(tb2, sb[raw][0], sinv, OP.mult, deps=[r0], eng=eng)
                    r1 = c.tt(sb[dst][1], tb, tb2, OP.add, deps=[a3, a4], eng=eng)
                    rop[nm] = (r0, r1)
                qtt, ktt = [], []
                for fc in range(2):
                    qtt.append(c.tt(sb["qt"][fc], sb["qf"][fc],
                                    sb["crossb"][h], OP.mult,
                                    deps=[rop["q"][fc]] + wg_prev.get("qtst", [])))
                    ktt.append(c.tt(sb["ktd"][fc], sb["kf"][fc],
                                    sb["kdecb"][h], OP.mult, deps=[rop["k"][fc]],
                                    eng="gpsimd"))
                qtst = []
                for fc in range(2):
                    qtst.append(c.store(
                        c.ap["qt_sp"][h * 256 + fc * 128: h * 256 + (fc + 1) * 128, sl],
                        sb["qt"][fc], deps=[qtt[fc]]))
                wg_prev["qtst"] = qtst
                ktmt = []
                for js in range(4):
                    t0_ = c.mm(4, sb["ktd"][0][:, js * 128:(js + 1) * 128], sb["ident_b"],
                               True, True, deps=[ktt[0]], cols=128, coff=0)
                    t1_ = c.mm(4, sb["ktd"][1][:, js * 128:(js + 1) * 128], sb["ident_b"],
                               True, True, deps=[ktt[1]], cols=128, coff=128)
                    e = c.cp(sb["ktm"][js], c.PS[4][:, 0:256], deps=[t0_, t1_],
                             eng="gpsimd")
                    c.consume(4, e)
                    ktmt.append(e)
                att = []
                for js in range(4):
                    t = None
                    for fc in range(2):
                        t = c.mm(2, sb["kf"][fc][:, js * 128:(js + 1) * 128], sb["qf"][fc],
                                 fc == 0, fc == 1, deps=[rop["k"][fc], rop["q"][fc]])
                    e = c.tt(sb["aT"][js], c.PS[2][:, :], sb["maskT"][h][js],
                             OP.mult, deps=[t])
                    c.consume(2, e)
                    att.append(e)
                sbf = []
                for dkc in range(2):
                    e = c.cp(sb["Sbf"][dkc], sb["Sst"][h][dkc],
                             deps=wg_prev.get("sbf", []), eng="gpsimd")
                    sbf.append(e)
                oprev = wg_prev.get("oev", {})
                for dvs in range(4):
                    bank = (0, 1, 3, 5)[dvs]
                    t = None
                    for js in range(4):
                        t = c.mm(bank, sb["vtm"][js][:, dvs * 128:(dvs + 1) * 128],
                                 sb["aT"][js], js == 0, False, deps=[vt[js], att[js]])
                    for dkc in range(2):
                        t = c.mm(bank, sb["Sbf"][dkc][:, dvs * 128:(dvs + 1) * 128],
                                 sb["qt"][dkc], False, dkc == 1, deps=[sbf[dkc], qtt[dkc]])
                    ev = c.cp(sb["oev"][dvs % 2], c.PS[bank][:, :],
                              deps=[t] + oprev.get(dvs % 2, []),
                              eng="vector" if dvs % 2 == 0 else "gpsimd")
                    c.consume(bank, ev)
                    st = c.store(c.ap["o_sp"][h * 512 + dvs * 128: h * 512 + (dvs + 1) * 128, sl],
                                 sb["oev"][dvs % 2], deps=[ev])
                    oprev[dvs % 2] = [st]
                    wg_prev.setdefault("sbf", []).clear() if False else None
                wg_prev["oev"] = oprev
                wg_prev["sbf"] = [t]
                for dks in range(2):
                    bank = (2, 4)[dks]
                    t = None
                    for js in range(4):
                        t = c.mm(bank, sb["ktm"][js][:, dks * 128:(dks + 1) * 128],
                                 sb["vtm"][js], js == 0, js == 3, deps=[ktmt[js], vt[js]])
                    zt = c.stt(sb["Sst"][h][dks], sb["Sst"][h][dks], GAM[h] ** CH,
                               c.PS[bank][:, :], OP.mult, OP.add, deps=[t, sbf[dks]])
                    c.consume(bank, zt)
            p.barrier()

        sends = []
        for h in range(4):
            for dkc in range(2):
                sends.append(c.store(
                    c.ap["sndA"][h * 256 + dkc * 128: h * 256 + (dkc + 1) * 128, :],
                    sb["Sst"][h][dkc], deps=[]))
        g = c.gather(c.ap["sndA"], c.ap["rcvA"], deps=sends)
        p.barrier()

    with c.phase() as es:
        c.alloc(es, "Sin", (4, 2, 128, CH), F32)
        c.alloc(es, "Sbf2", (2, 128, CH), BF16)
        c.alloc(es, "qt2", (2, 128, CH), BF16)
        c.alloc(es, "og_", (16, 128, CH), BF16)
        c.alloc(es, "gsl", (4, 128, CH), BF16)
        c.alloc(es, "Wd", (KC, 128, 2048), BF16)
        c.alloc(es, "We", (16, 128, 1024), BF16)
        first = [g]
        sl_ticks = []
        for h in range(4):
            for dkc in range(2):
                l = c.load(sb["Sin"][h][dkc],
                           c.ap["rcvA"][0][h * 256 + dkc * 128: h * 256 + (dkc + 1) * 128, :],
                           deps=first, q="gpsimd")
                first = []
                sl_ticks.append(l)
        for h in range(4):
            for dkc in range(2):
                c.ts(sb["Sin"][h][dkc], sb["Sin"][h][dkc], sb["is_sec_c"][:, 0:1],
                     None, OP.mult, deps=[sl_ticks[-1]])
        for kc in range(KC):
            c.load(sb["Wd"][kc], c.ap["l0a_wg"][kc * 128:(kc + 1) * 128, :])
        for dvc in range(16):
            c.load(sb["We"][dvc], c.ap["l0a_wo"][dvc * 128:(dvc + 1) * 128, :])
        p.barrier()

        for n in range(NCH):
            sl = slice(n * CH, (n + 1) * CH)
            ldf, ldb, tns = pre_ln(c, n, "x_in", deps=[])
            ognt = {}
            p2war = []
            for h in range(4):
                qtl = []
                first = list(p2war)
                for fc in range(2):
                    qtl.append(c.load(sb["qt2"][fc],
                                      c.ap["qt_sp"][h * 256 + fc * 128: h * 256 + (fc + 1) * 128, sl],
                                      deps=first, q="sync"))
                    first = []
                sbf = []
                for dkc in range(2):
                    sbf.append(c.cp(sb["Sbf2"][dkc], sb["Sin"][h][dkc], deps=p2war,
                                    eng="gpsimd"))
                opl = []
                for dvs in range(4):
                    opl.append(c.load(sb["og_"][h * 4 + dvs],
                                      c.ap["o_sp"][h * 512 + dvs * 128: h * 512 + (dvs + 1) * 128, sl],
                                      deps=[], q="sync"))
                odep = []
                for dvs in range(4):
                    bank = (0, 1, 3, 5)[dvs]
                    t = None
                    for dkc in range(2):
                        t = c.mm(bank, sb["Sbf2"][dkc][:, dvs * 128:(dvs + 1) * 128],
                                 sb["qt2"][dkc], dkc == 0, dkc == 1,
                                 deps=[sbf[dkc], qtl[dkc]])
                    e = c.tt(sb["og_"][h * 4 + dvs], c.PS[bank][:, :],
                             sb["og_"][h * 4 + dvs], OP.add, deps=[t, opl[dvs]])
                    c.consume(bank, e)
                    odep.append(e)
                    p2war = [t]
                if n < NCH - 1:
                    for dkc in range(2):
                        c.ts(sb["Sin"][h][dkc], sb["Sin"][h][dkc], GAM[h] ** CH, None,
                             OP.mult, deps=[sbf[dkc]], eng="gpsimd")
                t_sq = None
                sqd = []
                for dvs in range(4):
                    tq = c.act(sb["sq"], sb["og_"][h * 4 + dvs], AF.Square,
                               deps=[odep[dvs]] + sqd)
                    t_sq = c.mm(6, sb["ones_col_b"], sb["sq"], dvs == 0, dvs == 3,
                                deps=[tq], rows=1, cols=CH)
                    sqd = [t_sq]
                t1 = c.ts(sb["row"], c.PS[6][0:1, :], 1.0 / 512.0, None, OP.mult,
                          deps=[t_sq])
                c.consume(6, t1)
                t2a = c.act(sb["row"], sb["row"], AF.Sqrt, bias=1e-5, deps=[t1])
                t2 = c.recip(sb["row"], sb["row"], deps=[t2a])
                tb = c.bcast(6, sb["row"], [t2])
                e1 = c.cp(sb["bc1"], c.PS[6][:, :], deps=[tb])
                c.consume(6, e1)
                for dvs in range(4):
                    bank = (2, 4)[dvs % 2]
                    t = None
                    for kc in range(KC):
                        t = c.mm(bank,
                                 sb["Wd"][kc][:, h * 512 + dvs * 128: h * 512 + (dvs + 1) * 128],
                                 sb["xn"][kc], kc == 0, kc == KC - 1, deps=[tns[kc]])
                    gs = c.act(sb["gsl"][dvs], c.PS[bank][:, :], AF.Silu, deps=[t])
                    c.consume(bank, gs)
                    o1 = c.stt(sb["og_"][h * 4 + dvs], sb["og_"][h * 4 + dvs],
                               sb["gnw_c"][dvs][:, 0:1], sb["bc1"],
                               OP.mult, OP.mult, deps=[odep[dvs], e1])
                    ognt[h * 4 + dvs] = c.tt(sb["og_"][h * 4 + dvs],
                                             sb["og_"][h * 4 + dvs],
                                             sb["gsl"][dvs], OP.mult, deps=[o1, gs])
            ytick, ylist = [], []
            for ms in range(8):
                bank = (0, 1, 3, 5)[ms % 4]
                t = None
                for dvc in range(16):
                    t = c.mm(bank, sb["We"][dvc][:, ms * 128:(ms + 1) * 128],
                             sb["og_"][dvc], dvc == 0, dvc == 15, deps=[ognt[dvc]])
                e = c.cp(sb["ao"][ms], c.PS[bank][:, :], deps=[t],
                         eng="vector" if ms % 2 == 0 else "gpsimd")
                c.consume(bank, e)
                ytick.append(e)
                ylist.append(sb["ao"][ms])
            post_ln_residual(c, n, ylist, ytick, "x_res")
            p.barrier()


def l0_ffn(c):
    sb, p = c.sb, c.p
    with c.phase() as es:
        c.alloc(es, "Wq0", (KC, 128, 1024), BF16)
        c.alloc(es, "rtw", (KC, 128, 256), BF16)
        c.alloc(es, "Wo0", (KC, 128, 1024), BF16)
        c.alloc(es, "lvw", (2, 128, 1024), BF16)
        c.alloc(es, "treeP_b", (4, 128, 256), BF16)
        c.alloc(es, "qful", (KC, 128, CH), BF16)
        c.alloc(es, "u_b", (4, 128, CH), BF16)
        c.alloc(es, "lp_b", (2, 128, CH), BF16)
        for kc in range(KC):
            c.load(sb["Wq0"][kc], c.ap["l0f_wq"][kc * 128:(kc + 1) * 128, :])
            c.load(sb["rtw"][kc], c.ap["l0f_rt"][kc * 128:(kc + 1) * 128, :])
            c.load(sb["Wo0"][kc], c.ap["l0f_wo"][kc * 128:(kc + 1) * 128, :])
        for mc in range(2):
            c.load(sb["lvw"][mc], c.ap["l0f_lv"][mc * 128:(mc + 1) * 128, :])
        for uc in range(4):
            c.load(sb["treeP_b"][uc], c.ap["treeP"][uc * 128:(uc + 1) * 128, :])
        p.barrier()

        for n in range(NCH):
            ldf, ldb, tns = pre_ln(c, n, "x_res", deps=[])
            qt = []
            for fc in range(KC):
                bank = (0, 1)[fc % 2]
                t = None
                for kc in range(KC):
                    t = c.mm(bank, sb["Wq0"][kc][:, fc * 128:(fc + 1) * 128], sb["xn"][kc],
                             kc == 0, kc == KC - 1, deps=[tns[kc]])
                e = c.cp(sb["qful"][fc], c.PS[bank][:, :], deps=[t],
                         eng="vector" if fc % 2 == 0 else "gpsimd")
                c.consume(bank, e)
                qt.append(e)
            lg = []
            for ms2 in range(2):
                bank = (2, 3)[ms2]
                t = None
                for ec in range(KC):
                    t = c.mm(bank, sb["rtw"][ec][:, ms2 * 128:(ms2 + 1) * 128],
                             sb["qful"][ec], ec == 0, ec == KC - 1, deps=[qt[ec]])
                lg.append(t)
            prev_read = []
            udep = []
            for ms2 in range(2):
                bank = (2, 3)[ms2]
                a1 = c.act(sb["tmp"], c.PS[bank][:, :], AF.Abs, deps=[lg[ms2]] + prev_read)
                a2 = c.act(sb["tmp"], sb["tmp"], AF.Exp, scale=-1.0, deps=[a1])
                a3 = c.act(sb["tmp"], sb["tmp"], AF.Ln, bias=1.0, deps=[a2])
                a4 = c.act(sb["tmpg"], c.PS[bank][:, :], AF.Relu, deps=[lg[ms2]] + prev_read)
                s1 = c.tt(sb["tmpg"], sb["tmpg"], sb["tmp"], OP.add, deps=[a3, a4])
                s2 = c.ts(sb["u_b"][2 + ms2], sb["tmpg"], -1.0, None, OP.mult, deps=[s1])
                s3 = c.tt(sb["u_b"][ms2], c.PS[bank][:, :], sb["tmpg"], OP.subtract,
                          deps=[s1])
                c.consume(bank, s3)
                prev_read = [s1, s3, s2]
                udep += [s2, s3]
            lpt = []
            for ms2 in range(2):
                bank = (4, 5)[ms2]
                t = None
                for uc in range(4):
                    t = c.mm(bank, sb["treeP_b"][uc][:, ms2 * 128:(ms2 + 1) * 128],
                             sb["u_b"][uc], uc == 0, uc == 3, deps=udep)
                e = c.act(sb["lp_b"][ms2], c.PS[bank][:, :], AF.Exp, deps=[t])
                c.consume(bank, e)
                lpt.append(e)
            rt = []
            for ds in range(8):
                bank = (0, 1)[ds % 2]
                t = None
                for mc in range(2):
                    t = c.mm(bank, sb["lvw"][mc][:, ds * 128:(ds + 1) * 128],
                             sb["lp_b"][mc], mc == 0, mc == 1, deps=[lpt[mc], qt[ds]])
                e = c.cp(sb["qful"][ds], c.PS[bank][:, :], deps=[t],
                         eng="vector" if ds % 2 == 0 else "gpsimd")
                c.consume(bank, e)
                rt.append(e)
            ytick, ylist = [], []
            for es_ in range(8):
                bank = (2, 3)[es_ % 2]
                t = None
                for dc in range(KC):
                    t = c.mm(bank, sb["Wo0"][dc][:, es_ * 128:(es_ + 1) * 128],
                             sb["qful"][dc], dc == 0, dc == KC - 1, deps=[rt[dc]])
                e = c.ts(sb["ao"][es_], c.PS[bank][:, :],
                         sb["l0f_wb_c"][es_][:, 0:1], None, OP.add, deps=[t])
                c.consume(bank, e)
                ytick.append(e)
                ylist.append(sb["ao"][es_])
            post_ln_residual(c, n, ylist, ytick, "x_res")
            p.barrier()


def l1_attn(c):
    sb, p = c.sb, c.p
    with c.phase() as es:
        c.alloc(es, "Wf", (KC, 128, 3072), BF16)
        c.alloc(es, "Wog", (KC, 128, 1024), BF16)
        c.alloc(es, "wab_s", (KC, 128, 8), BF16)
        c.alloc(es, "qkv", (24, 128, CH + 3), BF16)
        c.alloc(es, "cv", (2, 128, CH), BF16)
        c.alloc(es, "ogc", (128, CH), BF16)
        c.alloc(es, "halo_xn", (KC, 128, 3), BF16)
        for kc in range(KC):
            c.load(sb["Wf"][kc], c.ap["l1a_win"][kc * 128:(kc + 1) * 128, :])
            c.load(sb["Wog"][kc], c.ap["l1a_wog"][kc * 128:(kc + 1) * 128, :])
            c.load(sb["wab_s"][kc], c.ap["l1a_wab"][kc * 128:(kc + 1) * 128, :])
        p.barrier()

        ldf, ldb, tns = pre_ln(c, NCH - 1, "x_res", deps=[], want_f32=False)
        sends = []
        for kc in range(KC):
            sends.append(c.store(c.ap["sndA2"][kc * 128:(kc + 1) * 128, :],
                                 sb["xn"][kc][:, CH - 3:CH], deps=[tns[kc]]))
        g2 = c.gather(c.ap["sndA2"], c.ap["rcvA2"], deps=sends)
        hx = []
        first = [g2]
        hl = []
        for kc in range(KC):
            hl.append(c.load(sb["xf"][kc][:, 0:3],
                             c.ap["rcvA2"][0][kc * 128:(kc + 1) * 128, :],
                             deps=first, q="gpsimd"))
            first = []
        for kc in range(KC):
            hx.append(c.ts(sb["halo_xn"][kc], sb["xf"][kc][:, 0:3],
                           sb["is_sec_c"][:, 0:1], None, OP.mult, deps=[hl[-1]]))
        halo_pr = []
        for fs in range(24):
            t = None
            for kc in range(KC):
                t = c.mm(5, sb["Wf"][kc][:, fs * 128:(fs + 1) * 128], sb["halo_xn"][kc],
                         kc == 0, kc == KC - 1, deps=[hx[kc]], cols=3)
            e = c.cp(sb["qkv"][fs][:, 0:3], c.PS[5][:, 0:3], deps=[t], eng="gpsimd")
            c.consume(5, e)
            halo_pr.append(e)
        p.barrier()

        ablg_prev = []
        for n in range(NCH):
            t0, t1 = n * CH, (n + 1) * CH
            sl = slice(t0, t1)
            ldf, ldb, tns = pre_ln(c, n, "x_res", deps=[], want_f32=False)
            t = None
            for kc in range(KC):
                t = c.mm(5, sb["wab_s"][kc], sb["xn"][kc], kc == 0, kc == KC - 1,
                         deps=[tns[kc]], rows=8)
            aa = c.act(sb["abr_c"][0:8, :], c.PS[5][0:8, :], AF.Sigmoid,
                       bias=sb["bab_c"][0:8, 0:1], deps=[t] + ablg_prev)
            c.consume(5, aa)
            ab2 = c.ts(sb["abr_c"][0:4, :], sb["abr_c"][0:4, :], 0.98, 0.01,
                       OP.mult, OP.add, deps=[aa])
            bb = ab2
            la = c.act(sb["lga_c"][0:4, :], sb["abr_c"][0:4, :], AF.Ln,
                       deps=[ab2] + ablg_prev)
            ablg_prev = [c.store(c.ap["ab_sp"][:, sl], sb["abr_c"][0:8, :],
                                 deps=[ab2, bb]),
                         c.store(c.ap["lg_sp"][:, sl], sb["lga_c"][0:4, :], deps=[la])]
            ogst = []
            for kc8 in range(KC):
                bank = (2, 3)[kc8 % 2]
                t = None
                for kc in range(KC):
                    t = c.mm(bank, sb["Wog"][kc][:, kc8 * 128:(kc8 + 1) * 128],
                             sb["xn"][kc], kc == 0, kc == KC - 1, deps=[tns[kc]])
                e = c.act(sb["ogc"], c.PS[bank][:, :], AF.Sigmoid,
                          bias=sb["bog_c"][kc8][:, 0:1], deps=[t] + ogst)
                st = c.store(c.ap["og_sp"][kc8 * 128:(kc8 + 1) * 128, sl], sb["ogc"],
                             deps=[e])
                ogst = [st]
                c.consume(bank, st)
            cvst = {}
            prt = []
            for fs in range(24):
                bank = (0, 1)[fs % 2]
                t = None
                for kc in range(KC):
                    t = c.mm(bank, sb["Wf"][kc][:, fs * 128:(fs + 1) * 128], sb["xn"][kc],
                             kc == 0, kc == KC - 1, deps=[tns[kc]])
                e = c.cp(sb["qkv"][fs][:, 3:3 + CH], c.PS[bank][:, :],
                         deps=[t] + (halo_pr if n == 0 else []),
                         eng="vector" if fs % 2 == 0 else "gpsimd")
                c.consume(bank, e)
                prt.append(e)
            for gi, (grp, base, dstn) in enumerate((("q", 0, "q_sp"), ("k", 8, "k_sp"),
                                                    ("v", 16, "v_sp"))):
                for h in range(4):
                    sil = [None, None]
                    for fc in range(2):
                        fs = base + 2 * h + fc
                        eng = "vector" if fs % 2 == 0 else "gpsimd"
                        tf = sb["tmp"] if eng == "vector" else sb["tmpg"]

                        def wcol(tau, fs=fs, gi=gi):
                            return sb["cw3"][gi][fs % 8][:, tau:tau + 1]
                        tcv = c.ts(tf, sb["qkv"][fs][:, 0:CH], wcol(0), None, OP.mult,
                                   deps=[prt[fs]], eng=eng)
                        for tau in range(1, 4):
                            tcv = c.stt(tf, sb["qkv"][fs][:, tau:tau + CH], wcol(tau), tf,
                                        OP.mult, OP.add, deps=[tcv], eng=eng)
                        sil[fc] = c.act(sb["cv"][fc], tf, AF.Silu,
                                        deps=[tcv] + cvst.get(fc, []))
                        c.cp(sb["qkv"][fs][:, 0:3], sb["qkv"][fs][:, CH:CH + 3],
                             deps=[sil[fc]], eng=eng)
                    if grp in ("q", "k"):
                        t_sq = None
                        sqd = []
                        for fc in range(2):
                            tq = c.act(sb["sq"], sb["cv"][fc], AF.Square,
                                       deps=[sil[fc]] + sqd)
                            t_sq = c.mm(6, sb["ones_col_b"], sb["sq"], fc == 0, fc == 1,
                                        deps=[tq], rows=1, cols=CH)
                            sqd = [t_sq]
                        r1 = c.act(sb["row"], c.PS[6][0:1, :], AF.Sqrt, deps=[t_sq])
                        c.consume(6, r1)
                        r2 = c.ts(sb["row"], sb["row"], 1e-6, None, OP.max, deps=[r1])
                        r3 = c.recip(sb["row"], sb["row"], deps=[r2])
                        tb = c.bcast(6, sb["row"], [r3])
                        e1 = c.cp(sb["bc1"], c.PS[6][:, :], deps=[tb])
                        c.consume(6, e1)
                        for fc in range(2):
                            m = c.tt(sb["cv"][fc], sb["cv"][fc], sb["bc1"], OP.mult,
                                     deps=[sil[fc], e1],
                                     eng="vector" if fc == 0 else "gpsimd")
                            cvst[fc] = [c.store(
                                c.ap[dstn][h * 256 + fc * 128: h * 256 + (fc + 1) * 128, sl],
                                sb["cv"][fc], deps=[m])]
                    else:
                        for fc in range(2):
                            cvst[fc] = [c.store(
                                c.ap[dstn][h * 256 + fc * 128: h * 256 + (fc + 1) * 128, sl],
                                sb["cv"][fc], deps=[sil[fc]])]
            p.barrier()

    for es_ in range(8):
        c.act(sb["rsig"][es_], sb["ramp_c"][es_][:, 0:1], AF.Sigmoid, deps=[])

    with c.phase() as es:
        c.alloc(es, "kq2", (2, 4, 128, 128), BF16)
        histT = [[], []]
        prevT = []
        for ci in range(NDC):
            t0, t1 = ci * DC, (ci + 1) * DC
            for h in range(4):
                par = (ci * 4 + h) % 2
                sb["kq"] = sb["kq2"][par]
                prevT = histT[par]
                lds = []
                first = list(prevT)
                for fc in range(2):
                    lds.append(c.load(sb["kq"][fc],
                                      c.ap["k_sp"][h * 256 + fc * 128: h * 256 + (fc + 1) * 128, t0:t1],
                                      deps=first, q="sync"))
                    first = []
                    lds.append(c.load(sb["kq"][2 + fc],
                                      c.ap["q_sp"][h * 256 + fc * 128: h * 256 + (fc + 1) * 128, t0:t1],
                                      deps=[], q="sync"))
                lds = [lds[-1]] * 4
                lrl = c.load(sb["lgrow"], c.ap["lg_sp"][h:h + 1, t0:t1], deps=prevT,
                             q="gpsimd")
                brl = c.load(sb["brow"], c.ap["ab_sp"][4 + h:5 + h, t0:t1], deps=[],
                             q="gpsimd")
                lrl = brl
                s1 = c.scan(sb["lrow"], sb["lgrow"], 0.0, OP.add, deps=[lrl] + prevT)
                tA = c.rowT(0, sb["lrow"], [s1], coff=0)
                e1 = c.cp(sb["lac"], c.PS[0][:, 0:1], deps=[tA])
                tC = c.mm(0, sb["ones_row_f"][0:1, :], sb["lrow"][0:1, 127:128], True,
                          True, deps=[s1, e1], cols=1, coff=1)
                elb = c.cp(sb["lacb"], c.PS[0][:, 1:2], deps=[tC])
                c.consume(0, elb)
                tB = c.bcast(1, sb["lrow"], [s1], cols=128)
                tb_ = c.rowT(2, sb["brow"], [brl], coff=0)
                eb = c.cp(sb["bcol"], c.PS[2][:, 0:1], deps=[tb_])
                bbc = c.bcast(2, sb["brow"], [brl, eb], cols=128, coff=128)
                tkk = tkq = None
                for fc in range(2):
                    tkk = c.mm(3, sb["kq"][fc], sb["kq"][fc], fc == 0, fc == 1,
                               deps=[lds[2 * fc]], cols=128)
                    tkq = c.mm(4, sb["kq"][fc], sb["kq"][2 + fc], fc == 0, fc == 1,
                               deps=[lds[2 * fc], lds[2 * fc + 1]], cols=128)
                d1 = c.ts(sb["m128"], c.PS[1][:, 0:128], sb["lac"][:, 0:1], None,
                          OP.subtract, deps=[tB, e1])
                d2 = c.tt(sb["m128"], sb["m128"], sb["maddT_f"], OP.add, deps=[d1])
                gpt = c.act(sb["gpt"], sb["m128"], AF.Exp, deps=[d2])
                gst = c.tt(sb["tmpb"][:, 0:128], sb["gpt"], sb["ident_b"], OP.subtract,
                           deps=[gpt])
                d3 = c.ts(sb["m128b"], c.PS[1][:, 0:128], sb["lac"][:, 0:1], None,
                          OP.subtract, deps=[tB, e1])
                c.consume(1, d3)
                d4 = c.tt(sb["m128b"], sb["m128b"], sb["madl_f"], OP.add, deps=[d3])
                gpu = c.act(sb["gpu"], sb["m128b"], AF.Exp, scale=-1.0, deps=[d4])
                gsu = c.tt(sb["tmpb2"][:, 0:128], sb["gpu"], sb["ident_b"], OP.subtract,
                           deps=[gpu])
                m1 = c.tt(sb["m128"], c.PS[3][:, 0:128], sb["tmpb"][:, 0:128], OP.mult,
                          deps=[tkk, gst])
                ebb = c.cp(sb["bc1"][:, 0:128], c.PS[2][:, 128:256], deps=[bbc])
                c.consume(2, ebb)
                m2 = c.tt(sb["m128"], sb["m128"], sb["bc1"][:, 0:128], OP.mult,
                          deps=[m1, ebb])
                mmt = c.tt(sb["mmt"], sb["m128"], sb["ident_f"], OP.add, deps=[m2])
                x0t = c.tt(sb["nxt"], sb["ident_f"], sb["m128"], OP.subtract, deps=[m2])
                m3 = c.tt(sb["m128b"], c.PS[3][:, 0:128], sb["tmpb2"][:, 0:128], OP.mult,
                          deps=[tkk, gsu])
                c.consume(3, m3)
                m4 = c.ts(sb["m128b"], sb["m128b"], sb["bcol"][:, 0:1], None, OP.mult,
                          deps=[m3, eb])
                x0 = c.tt(sb["nx"], sb["ident_f"], sb["m128b"], OP.subtract, deps=[m4])
                wt_ = c.tt(sb["tmpb"][:, 0:128], c.PS[4][:, 0:128], sb["gpt"], OP.mult,
                           deps=[tkq, gpt])
                c.consume(4, wt_)
                stw = c.store(c.ap["wt_sp"][(ci * 4 + h) * 128:(ci * 4 + h + 1) * 128, :],
                              sb["tmpb"][:, 0:128], deps=[wt_])
                xa, xat = "nx", "nxt"
                xb_, xbt = "nx2", "nxt2"
                tx, txt = x0, x0t
                for it in range(6):
                    ty = c.mm(5, sb["mmt"], sb[xa], True, True, deps=[mmt, tx], cols=128)
                    tz = c.tt(sb["zz"], sb["ident2_f"], c.PS[5][:, 0:128], OP.subtract,
                              deps=[ty])
                    c.consume(5, tz)
                    tX = c.mm(5, sb[xat], sb["zz"], True, True, deps=[txt, tz], cols=128)
                    ex = c.cp(sb[xb_], c.PS[5][:, 0:128], deps=[tX])
                    c.consume(5, ex)
                    tXT = c.mm(5, sb["zz"], sb[xat], True, True, deps=[tz, txt], cols=128)
                    ext = c.cp(sb[xbt], c.PS[5][:, 0:128], deps=[tXT])
                    c.consume(5, ext)
                    xa, xb_ = xb_, xa
                    xat, xbt = xbt, xat
                    tx, txt = ex, ext
                stt_ = c.store(c.ap["tm_sp"][(ci * 4 + h) * 128:(ci * 4 + h + 1) * 128, :],
                               sb[xat], deps=[txt])
                dcc = c.act(sb["dccol"], sb["lac"], AF.Exp, scale=-1.0,
                            bias=sb["lacb"][:, 0:1], deps=[e1, elb])
                tk0 = c.mm(0, sb["kq"][0], sb["ident_b"], True, True, deps=[elb],
                           cols=128, coff=0)
                tk1 = c.mm(0, sb["kq"][1], sb["ident_b"], True, True, deps=[],
                           cols=128, coff=128)
                ek = c.ts(sb["tmpb2"][:, 0:256], c.PS[0][:, 0:256], sb["dccol"][:, 0:1],
                          None, OP.mult, deps=[tk0, tk1, dcc])
                c.consume(0, ek)
                stk = c.store(c.ap["ktd_sp"][(h * NDC + ci) * 128:(h * NDC + ci + 1) * 128, :],
                              sb["tmpb2"][:, 0:256], deps=[ek])
                histT[par] = [tA, tC, tB, tb_, stw, stt_, stk, s1, ek, wt_, gst,
                              gsu, d4]
        p.barrier()

    for h in range(4):
        for kc2 in range(2):
            p.emit("vector", "memset", sb["Zst"][h][kc2][:, :], 0.0, deps=[])
    p.barrier()
    l1_sweep(c, "Zst", correction=False)
    p.barrier()

    sends = []
    for h in range(4):
        for kc2 in range(2):
            sends.append(c.store(
                c.ap["sndB"][h * 256 + kc2 * 128: h * 256 + (kc2 + 1) * 128, :],
                sb["Zst"][h][kc2], deps=[]))
    g = c.gather(c.ap["sndB"], c.ap["rcvB"], deps=sends)
    first = [g]
    zl_ticks = []
    for h in range(4):
        for kc2 in range(2):
            l = c.load(sb["Zin"][h][kc2],
                       c.ap["rcvB"][0][h * 256 + kc2 * 128: h * 256 + (kc2 + 1) * 128, :],
                       deps=first, q="gpsimd")
            first = []
            zl_ticks.append(l)
    for h in range(4):
        for kc2 in range(2):
            c.ts(sb["Zin"][h][kc2], sb["Zin"][h][kc2], sb["is_sec_c"][:, 0:1],
                 None, OP.mult, deps=[zl_ticks[-1]])
    p.barrier()
    l1_sweep(c, "Zin", correction=True)
    p.barrier()


def l1_sweep(c, zname, correction):
    sb, p = c.sb, c.p
    with c.phase() as es:
        c.alloc(es, "kq2", (2, 4, 128, 128), BF16)
        c.alloc(es, "vfc2", (2, 2, 128, 128), BF16)
        c.alloc(es, "tmT2", (2, 128, 128), BF16)
        c.alloc(es, "wT2", (2, 128, 128), BF16)
        c.alloc(es, "ktdc2", (2, 128, 256), BF16)
        c.alloc(es, "rfm", (KC, 128, CH), BF16)
        if correction:
            c.alloc(es, "Wout", (KC, 128, 1024), BF16)
            c.alloc(es, "ogg", (KC, 128, CH), BF16)
            for kc in range(KC):
                c.load(sb["Wout"][kc], c.ap["l1a_wout"][kc * 128:(kc + 1) * 128, :])
        p.barrier()
        rfm_t = {}
        carry = {}
        for ci in range(NDC):
            t0, t1 = ci * DC, (ci + 1) * DC
            for h in range(4):
                par = (ci * 4 + h) % 2
                sb["kq"] = sb["kq2"][par]
                sb["vfc"] = sb["vfc2"][par]
                sb["tmT"] = sb["tmT2"][par]
                sb["wT"] = sb["wT2"][par]
                sb["ktdc"] = sb["ktdc2"][par]
                pv = carry.get(("ld", par), [])
                lds = []
                first = list(pv)
                for fc in range(2):
                    lds.append(c.load(sb["kq"][fc],
                                      c.ap["k_sp"][h * 256 + fc * 128: h * 256 + (fc + 1) * 128, t0:t1],
                                      deps=first, q="sync"))
                    first = []
                    lds.append(c.load(sb["kq"][2 + fc],
                                      c.ap["q_sp"][h * 256 + fc * 128: h * 256 + (fc + 1) * 128, t0:t1],
                                      deps=[], q="sync"))
                ltm = c.load(sb["tmT"],
                             c.ap["tm_sp"][(ci * 4 + h) * 128:(ci * 4 + h + 1) * 128, :],
                             deps=[], q="sync")
                lwt = c.load(sb["wT"],
                             c.ap["wt_sp"][(ci * 4 + h) * 128:(ci * 4 + h + 1) * 128, :],
                             deps=[], q="sync")
                lkt = c.load(sb["ktdc"],
                             c.ap["ktd_sp"][(h * NDC + ci) * 128:(h * NDC + ci + 1) * 128, :],
                             deps=[], q="sync")
                lds = [lkt] * 4
                ltm = lwt = lkt
                lrl = c.load(sb["lgrow"], c.ap["lg_sp"][h:h + 1, t0:t1],
                             deps=carry.get("rowr", []), q="gpsimd")
                brl = c.load(sb["brow"], c.ap["ab_sp"][4 + h:5 + h, t0:t1],
                             deps=[], q="gpsimd")
                lrl = brl
                s1 = c.scan(sb["lrow"], sb["lgrow"], 0.0, OP.add,
                            deps=[lrl] + carry.get("lrowr", []))
                tA = c.rowT(6, sb["lrow"], [s1], coff=0)
                e1 = c.cp(sb["lac"], c.PS[6][:, 0:1], deps=[tA] + carry.get("lacr", []))
                tC = c.mm(6, sb["ones_row_f"][0:1, :], sb["lrow"][0:1, 127:128], True,
                          True, deps=[s1, e1], cols=1, coff=1)
                elb = c.cp(sb["lacb"], c.PS[6][:, 1:2], deps=[tC])
                tb_ = c.rowT(6, sb["brow"], [brl, elb], coff=2)
                eb = c.cp(sb["bcol"], c.PS[6][:, 2:3], deps=[tb_] + carry.get("bcolr", []))
                c.consume(6, eb)
                a1 = c.act(sb["acol"], sb["lac"], AF.Exp, deps=[e1] + carry.get("acolr", []))
                a2 = c.act(sb["accol"], sb["lacb"], AF.Exp,
                           deps=[elb] + carry.get("accolr", []))
                carry["rowr"] = [tA, tC, tb_]
                carry["lrowr"] = [tA, tC]
                carry["lacr"] = [a1]
                carry["bcolr"] = []
                carry["accolr"] = []
                Z = sb[zname][h]
                zbf = []
                for kc2 in range(2):
                    zbf.append(c.cp(sb["Zbf"][kc2], Z[kc2],
                                    deps=carry.get(("z", h, kc2), []) + pv,
                                    eng="gpsimd"))
                tkz = tqz = None
                for kc2 in range(2):
                    tkz = c.mm(0, sb["kq"][kc2], sb["Zbf"][kc2], kc2 == 0, kc2 == 1,
                               deps=[zbf[kc2], lds[2 * kc2]], cols=256)
                    tqz = c.mm(1, sb["kq"][2 + kc2], sb["Zbf"][kc2], kc2 == 0, kc2 == 1,
                               deps=[zbf[kc2], lds[2 * kc2 + 1]], cols=256)
                if not correction:
                    lv0 = c.load(sb["vfc"][0],
                                 c.ap["v_sp"][h * 256:h * 256 + 128, t0:t1],
                                 deps=pv, q="sync")
                    lv1 = c.load(sb["vfc"][1],
                                 c.ap["v_sp"][h * 256 + 128:h * 256 + 256, t0:t1],
                                 deps=[], q="sync")
                    lv0 = lv1
                    tv0 = c.mm(2, sb["vfc"][0], sb["ident_b"], True, True, deps=[lv0],
                               cols=128, coff=0)
                    tv1 = c.mm(2, sb["vfc"][1], sb["ident_b"], True, True, deps=[lv1],
                               cols=128, coff=128)
                    ev = c.cp(sb["vtmt"], c.PS[2][:, 0:256], deps=[tv0, tv1],
                              eng="gpsimd")
                    c.consume(2, ev)
                    r1 = c.ts(sb["tmp"][:, 0:256], c.PS[0][:, 0:256], sb["acol"][:, 0:1],
                              None, OP.mult, deps=[tkz, a1])
                    c.consume(0, r1)
                    r2 = c.tt(sb["tmp"][:, 0:256], sb["vtmt"], sb["tmp"][:, 0:256],
                              OP.subtract, deps=[r1, ev])
                    r3 = c.ts(sb["rhs_b"], sb["tmp"][:, 0:256], sb["bcol"][:, 0:1], None,
                              OP.mult, deps=[r2, eb])
                else:
                    ba = c.tt(sb["bacol"], sb["bcol"], sb["acol"], OP.mult, deps=[eb, a1])
                    r3 = c.ts(sb["rhs_b"], c.PS[0][:, 0:256], sb["bacol"][:, 0:1], None,
                              OP.mult, deps=[tkz, ba])
                    c.consume(0, r3)
                tu = c.mm(2, sb["tmT"], sb["rhs_b"], True, True, deps=[r3, ltm], cols=256)
                eu = c.cp(sb["ub"], c.PS[2][:, 0:256], deps=[tu])
                c.consume(2, eu)
                trd = c.mm(3, sb["wT"], sb["ub"], True, True, deps=[eu, lwt], cols=256)
                eqz = c.cp(sb["tmpg"][:, 0:256], c.PS[1][:, 0:256],
                           deps=[tqz] + carry.get("tmpgr", []), eng="gpsimd")
                c.consume(1, eqz)
                sl_r = (slice(ci * DC, (ci + 1) * DC), slice(h * 256, (h + 1) * 256))
                if not correction:
                    rd = c.stt(sb["reads"], sb["tmpg"][:, 0:256], sb["acol"][:, 0:1],
                               c.PS[3][:, 0:256], OP.mult, OP.add,
                               deps=[eqz, trd, a1] + carry.get("readsw", []))
                    c.consume(3, rd)
                    st = c.store(c.ap["reads_sp"][sl_r[0], sl_r[1]], sb["reads"],
                                 deps=[rd])
                    carry["readsw"] = [st]
                    carry["tmpgr"] = [rd]
                    carry["acolr"] = [rd]
                else:
                    lr = c.load(sb["reads"], c.ap["reads_sp"][sl_r[0], sl_r[1]],
                                deps=carry.get("readsw", []), q="sync")
                    rd1 = c.stt(sb["tmpg"][:, 0:256], sb["tmpg"][:, 0:256],
                                sb["acol"][:, 0:1], c.PS[3][:, 0:256], OP.mult,
                                OP.subtract, deps=[eqz, trd, a1], eng="gpsimd")
                    c.consume(3, rd1)
                    rd = c.tt(sb["reads"], sb["reads"], sb["tmpg"][:, 0:256], OP.add,
                              deps=[rd1, lr])
                    carry["tmpgr"] = [rd]
                    carry["acolr"] = [rd1]
                tzs = []
                for kc2 in range(2):
                    tz = c.mm(4, sb["ktdc"][:, kc2 * 128:(kc2 + 1) * 128], sb["ub"],
                              True, True, deps=[eu, lkt], cols=256)
                    tzs.append(tz)
                    zt = c.stt(Z[kc2], Z[kc2], sb["accol"][:, 0:1], c.PS[4][:, 0:256],
                               OP.mult, OP.add if not correction else OP.subtract,
                               deps=[tz, a2, zbf[kc2]])
                    c.consume(4, zt)
                    carry[("z", h, kc2)] = [zt]
                    carry["accolr"].append(zt)
                carry[("ld", par)] = [tkz, tqz, tu, trd] + tzs +                     ([tv0, tv1] if not correction else [])
                if correction:
                    sq1 = c.act(sb["tmpb"][:, 0:256], sb["reads"], AF.Square,
                                accum=sb["arow"], deps=[rd] + carry.get("arowr", []))
                    rsa = c.act(sb["arow"], sb["arow"], AF.Sqrt, scale=1.0 / 256.0,
                                bias=1e-6, deps=[sq1])
                    rs = c.recip(sb["arow"], sb["arow"], deps=[rsa])
                    rn = c.ts(sb["readb"], sb["reads"], sb["arow"][:, 0:1], None,
                              OP.mult, deps=[rs] + carry.get("readbr", []))
                    carry["readsw"] = [rn]
                    rts = []
                    for fs in range(2):
                        tt_ = c.mm(5, sb["readb"][:, fs * 128:(fs + 1) * 128],
                                   sb["ident_b"], True, True, deps=[rn], cols=128)
                        e = c.cp(sb["rfm"][2 * h + fs][:, (ci % 4) * 128:(ci % 4 + 1) * 128],
                                 c.PS[5][:, 0:128], deps=[tt_])
                        c.consume(5, e)
                        rfm_t[(2 * h + fs, ci % 4)] = e
                        rts.append(tt_)
                    carry["readbr"] = rts
                    carry["arowr"] = [rn]
            if correction and ci % 4 == 3:
                ng = ci // 4
                slg = slice(ng * CH, (ng + 1) * CH)
                ogl = []
                first = carry.get("oggr", [])
                for kc in range(KC):
                    ogl.append(c.load(sb["ogg"][kc],
                                      c.ap["og_sp"][kc * 128:(kc + 1) * 128, slg],
                                      deps=first, q="gpsimd"))
                    first = []
                ogl = [ogl[-1]] * KC
                ldf, ldb, tns = pre_ln(c, ng, "x_res", deps=[])
                ytick, ylist = [], []
                oggr = []
                for es_ in range(8):
                    bank = (0, 1)[es_ % 2]
                    t = None
                    for kc in range(KC):
                        t = c.mm(bank, sb["Wout"][kc][:, es_ * 128:(es_ + 1) * 128],
                                 sb["rfm"][kc], kc == 0, kc == KC - 1,
                                 deps=[rfm_t[(kc, 0)], rfm_t[(kc, 1)], rfm_t[(kc, 2)],
                                       rfm_t[(kc, 3)]])
                    e = c.ts(sb["ao"][es_], c.PS[bank][:, :],
                             sb["bout_c"][es_][:, 0:1], sb["rsig"][es_][:, 0:1],
                             OP.add, OP.mult, deps=[t])
                    c.consume(bank, e)
                    g2 = c.tt(sb["ao"][es_], sb["ao"][es_], sb["ogg"][es_], OP.mult,
                              deps=[e, ogl[es_]])
                    oggr.append(g2)
                    ytick.append(g2)
                    ylist.append(sb["ao"][es_])
                carry["oggr"] = oggr
                post_ln_residual(c, ng, ylist, ytick, "x_res")
                p.barrier()
                carry = {}



def rowmax_fm(c, mxt):
    sb = c.sb
    tb = c.cp(sb["tmpb"], sb["tmp"], deps=[mxt])
    cols = []
    for j in range(4):
        tj = c.mm(6, sb["tmpb"][:, j * 128:(j + 1) * 128], sb["ident_b"], True, True,
                  deps=[tb], cols=128)
        e = c.p.emit("vector", "reduce_max", deps=[tj], out=sb["m128"][:, j:j + 1],
                     in_=c.PS[6][:, 0:128], axis=mybir.AxisListType.X)
        c.consume(6, e)
        cols.append(e)
    cb = c.cp(sb["tmpb"][:, 0:4], sb["m128"][:, 0:4], deps=cols)
    ts_ = []
    for j in range(4):
        ts_.append(c.mm(6, sb["tmpb"][:, j:j + 1], sb["ident_b"], True, True,
                        deps=[cb], rows=1, cols=128, coff=j * 128))
    er = c.cp(sb["row"], c.PS[6][0:1, 0:CH], deps=ts_)
    c.consume(6, er)
    return er


def l1_ffn(c):
    sb, p = c.sb, c.p

    def wquant(src_name, dst, nk, psb, sd):
        ncols = 4096 if nk == KC else 1024
        ngrp = ncols // 512
        s_t = None
        prev = []
        for kc in range(nk):
            l = c.load(sb["wtile"][:, 0:ncols],
                       c.ap[src_name][kc * 128:(kc + 1) * 128, :], q="sync", deps=prev)
            for j in range(ngrp):
                ta = c.act(sb["sq"], sb["wtile"][:, j * 512:(j + 1) * 512], AF.Abs,
                           deps=[l] + ([s_t] if s_t else []))
                s_t = c.mm(psb, sb["ones_col_b"], sb["sq"], (kc == 0 and j == 0),
                           (kc == nk - 1 and j == ngrp - 1), deps=[ta], rows=1, cols=512)
            prev = [l, s_t]
        e1 = c.cp(sb["row"], c.PS[psb][0:1, :], deps=[s_t])
        c.consume(psb, e1)
        r1 = c.redsum(sb["row2"][0:1, 0:1], sb["row"][0:1, :], deps=[e1])
        r2 = c.ts(sb["row2"][0:1, 0:1], sb["row2"][0:1, 0:1], 1.0 / (1024.0 * 4096.0),
                  None, OP.mult, deps=[r1])
        r3 = c.ts(sb["row2"][0:1, 0:1], sb["row2"][0:1, 0:1], 1e-5, None, OP.max,
                  deps=[r2])
        td = c.mm(psb, sb["ones_row_f"][0:1, :], sb["row2"][0:1, 0:1], True, True,
                  deps=[r3], cols=1)
        ed = c.cp(sb[sd], c.PS[psb][:, 0:1], deps=[td])
        r4 = c.recip(sb["row2"][0:1, 0:1], sb["row2"][0:1, 0:1], deps=[r3, td])
        tq = c.mm(psb, sb["ones_row_f"][0:1, :], sb["row2"][0:1, 0:1], True, True,
                  deps=[r4, ed], cols=1, coff=1)
        es1 = c.cp(sb["sqc"], c.PS[psb][:, 1:2], deps=[tq])
        c.consume(psb, es1)
        prev = [es1]
        last = None
        for kc in range(nk):
            l = c.load(sb["wtile"][:, 0:ncols],
                       c.ap[src_name][kc * 128:(kc + 1) * 128, :], q="sync", deps=prev)
            q1 = c.ts(sb["wtile"][:, 0:ncols], sb["wtile"][:, 0:ncols],
                      sb["sqc"][:, 0:1], None, OP.mult, deps=[l, es1])
            for j in range(ngrp):
                i1 = c.cp(sb["ii32"], sb["wtile"][:, j * 512:(j + 1) * 512], deps=[q1])
                i2 = c.cp(sb["tmp"], sb["ii32"], deps=[i1])
                last = c.ts(dst[kc][:, j * 512:(j + 1) * 512], sb["tmp"], -1.0, 1.0,
                            OP.max, OP.min, deps=[i2])
            prev = [last]
        return last

    hA = []
    hedge_t = []
    with c.phase() as es:
        c.alloc(es, "w1q", (KC, 128, 4096), BF16)
        c.alloc(es, "xiq", (KC, 128, CH), BF16)
        c.alloc(es, "wtile", (128, 4096), F32)
        WQ1 = wquant("l1f_w1", sb["w1q"], KC, 6, "s1d")
        p.barrier()
        for n in range(NCH):
            sl = slice(n * CH, (n + 1) * CH)
            ldf, ldb, tns = pre_ln(c, n, "x_res", deps=[], want_f32=False)
            mx = None
            for kc in range(KC):
                a = c.act(sb["sq"], sb["xn"][kc], AF.Abs,
                          deps=[tns[kc]] + ([mx] if mx else []))
                if kc == 0:
                    mx = c.cp(sb["tmp"], sb["sq"], deps=[a])
                else:
                    mx = c.tt(sb["tmp"], sb["tmp"], sb["sq"], OP.max, deps=[a, mx])
            er = rowmax_fm(c, mx)
            m1 = c.ts(sb["row"], sb["row"], 1e-5, None, OP.max, deps=[er])
            m2 = c.recip(sb["row2"], sb["row"], deps=[m1])
            m3 = c.ts(sb["row"], sb["row2"], 127.0, None, OP.mult, deps=[m2])
            tb1 = c.bcast(6, sb["row"], [m3])
            e1 = c.cp(sb["bc1"], c.PS[6][:, :], deps=[tb1])
            c.consume(6, e1)
            m4 = c.recip(sb["row2"], sb["row"], deps=[m3])
            tb2 = c.bcast(7, sb["row2"], [m4])
            e2 = c.cp(sb["bc2"], c.PS[7][:, :], deps=[tb2])
            c.consume(7, e2)
            xit = []
            for kc in range(KC):
                y1 = c.tt(sb["tmp"], sb["xn"][kc], sb["bc1"], OP.mult,
                          deps=[e1, tns[kc]])
                i1 = c.cp(sb["ii32"], sb["tmp"], deps=[y1])
                i2 = c.cp(sb["tmp"], sb["ii32"], deps=[i1])
                xit.append(c.ts(sb["xiq"][kc], sb["tmp"], -128.0, 127.0, OP.max, OP.min,
                                deps=[i2]))
            prev_st = []
            for hs in range(32):
                bank = (0, 1, 3, 5)[hs % 4]
                t = None
                for kc in range(KC):
                    t = c.mm(bank, sb["w1q"][kc][:, hs * 128:(hs + 1) * 128],
                             sb["xiq"][kc], kc == 0, kc == KC - 1, deps=[xit[kc], WQ1])
                d1 = c.tt(sb["tmpg"], c.PS[bank][:, :], sb["bc2"], OP.mult,
                          deps=[t, e2] + prev_st, eng="gpsimd")
                c.consume(bank, d1)
                d2 = c.ts(sb["tmpb"], sb["tmpg"], sb["s1d"][:, 0:1], None, OP.mult,
                          deps=[d1], eng="gpsimd")
                st = c.store(c.ap["h_sp"][hs * 128:(hs + 1) * 128, sl], sb["tmpb"],
                             deps=[d2])
                prev_st = [st]
                hA.append(st)
                if n == 0:
                    hedge_t.append(c.cp(sb["hedge"][hs][:, 0:1], sb["tmpb"][:, 0:1],
                                        deps=[d2], eng="gpsimd"))
                if n == NCH - 1:
                    hedge_t.append(c.cp(sb["hedge"][hs][:, 1:2],
                                        sb["tmpb"][:, CH - 1:CH],
                                        deps=[d2], eng="gpsimd"))
            p.barrier()
    HA = hA[-1]

    sc_ = []
    for hs in range(32):
        sc_.append(c.store(c.ap["sndC"][hs * 128:(hs + 1) * 128, :], sb["hedge"][hs],
                           deps=hedge_t))
    g = c.gather(c.ap["sndC"], c.ap["rcvC"], deps=sc_)
    hhl = []
    first = [g]
    for hs in range(32):
        l1 = c.load(sb["hhal"][hs][:, 0:1],
                    c.ap["rcvC"][0][hs * 128:(hs + 1) * 128, 1:2], deps=first, q="gpsimd")
        first = []
        l2 = c.load(sb["hhal"][hs][:, 1:2],
                    c.ap["rcvC"][1][hs * 128:(hs + 1) * 128, 0:1], deps=[], q="gpsimd")
        hhl.append((l2, l2))

    with c.phase() as es:
        c.alloc(es, "w2q", (32, 128, 1024), BF16)
        with c.phase() as es2:
            c.alloc(es2, "wtile", (128, 4096), F32)
            WQ2 = wquant("l1f_w2", sb["w2q"], 32, 7, "s2d")
            p.barrier()
        c.alloc(es, "hq", (32, 128, CH), BF16)
        c.alloc(es, "hwin", (128, CH + 2), BF16)
        for n in range(NCH):
            sl = slice(n * CH, (n + 1) * CH)
            ldf, ldb, tns = pre_ln(c, n, "x_res", deps=[])
            lo = max(0, n * CH - 1)
            hi = min(TC, (n + 1) * CH + 1)
            off = 1 if n == 0 else 0
            convt = []
            for hs in range(32):
                deps_h = [HA] + (convt[-1:] if convt else [])
                l = c.load(sb["hwin"][:, off:off + (hi - lo)],
                           c.ap["h_sp"][hs * 128:(hs + 1) * 128, lo:hi], deps=deps_h,
                           q="sync")
                if n == 0:
                    l = c.ts(sb["hwin"][:, 0:1], sb["hhal"][hs][:, 0:1],
                             sb["is_sec_c"][:, 0:1], None, OP.mult,
                             deps=[l, hhl[hs][0]], eng="gpsimd")
                if n == NCH - 1:
                    l = c.ts(sb["hwin"][:, CH + 1:CH + 2], sb["hhal"][hs][:, 1:2],
                             sb["is_fst_c"][:, 0:1], None, OP.mult,
                             deps=[l, hhl[hs][1]], eng="gpsimd")

                def wcol(tau, hs=hs):
                    return sb["l1f_cw_c"][hs][:, tau:tau + 1]
                tcv = c.ts(sb["tmp"], sb["hwin"][:, 0:CH], wcol(0), None, OP.mult,
                           deps=[l])
                for tau in range(1, 3):
                    tcv = c.stt(sb["tmp"], sb["hwin"][:, tau:tau + CH], wcol(tau),
                                sb["tmp"], OP.mult, OP.add, deps=[tcv])
                convt.append(c.act(sb["hq"][hs], sb["tmp"], AF.Silu,
                                   bias=sb["l1f_cb_c"][hs][:, 0:1], deps=[tcv]))
            mx = None
            for hs in range(32):
                a = c.act(sb["sq"], sb["hq"][hs], AF.Abs,
                          deps=[convt[hs]] + ([mx] if mx else []))
                if hs == 0:
                    mx = c.cp(sb["tmp"], sb["sq"], deps=[a])
                else:
                    mx = c.tt(sb["tmp"], sb["tmp"], sb["sq"], OP.max, deps=[a, mx])
            er = rowmax_fm(c, mx)
            m1 = c.ts(sb["row"], sb["row"], 1e-5, None, OP.max, deps=[er])
            m2 = c.recip(sb["row2"], sb["row"], deps=[m1])
            m3 = c.ts(sb["row"], sb["row2"], 127.0, None, OP.mult, deps=[m2])
            tb1 = c.bcast(6, sb["row"], [m3])
            e1 = c.cp(sb["bc1"], c.PS[6][:, :], deps=[tb1])
            c.consume(6, e1)
            m4 = c.recip(sb["row2"], sb["row"], deps=[m3])
            tb2 = c.bcast(7, sb["row2"], [m4])
            e2 = c.cp(sb["bc2"], c.PS[7][:, :], deps=[tb2])
            c.consume(7, e2)
            hqt = []
            for hs in range(32):
                y1 = c.tt(sb["tmp"], sb["hq"][hs], sb["bc1"], OP.mult,
                          deps=[e1, convt[hs]])
                i1 = c.cp(sb["ii32"], sb["tmp"], deps=[y1])
                i2 = c.cp(sb["tmp"], sb["ii32"], deps=[i1])
                hqt.append(c.ts(sb["hq"][hs], sb["tmp"], -128.0, 127.0, OP.max, OP.min,
                                deps=[i2]))
            ytick, ylist = [], []
            for es_ in range(8):
                bank = (0, 1, 3, 5)[es_ % 4]
                t = None
                for kc in range(32):
                    t = c.mm(bank, sb["w2q"][kc][:, es_ * 128:(es_ + 1) * 128],
                             sb["hq"][kc], kc == 0, kc == 31, deps=[hqt[kc], WQ2])
                d1 = c.tt(sb["tmpg"], c.PS[bank][:, :], sb["bc2"], OP.mult,
                          deps=[t, e2], eng="gpsimd")
                c.consume(bank, d1)
                d2 = c.ts(sb["ao"][es_], sb["tmpg"], sb["s2d"][:, 0:1], None, OP.mult,
                          deps=[d1], eng="gpsimd")
                ytick.append(d2)
                ylist.append(sb["ao"][es_])
            post_ln_residual(c, n, ylist, ytick, "out")
            p.barrier()


def emit_all(c):
    sb, p = c.sb, c.p
    cst = []
    cst.append(c.load(sb["cvals_sb"][:], c.ap["cvals"]))
    pairs = [("ident_f", "ident"), ("ident2_f", "ident2"), ("maddT_f", "maddT"),
             ("madl_f", "madd_low"), ("ones_row_f", "ones_row"),
             ("is_sec_c", "is_sec"), ("is_fst_c", "is_fst"),
             ("one_f", "one"), ("ident_b", "ident"), ("ones_col_b", "ones_col"),
             ("bab_c", "l1a_bab")]
    for dst, src in pairs:
        cst.append(c.load(sb[dst][:], c.ap[src]))
    for i in range(4):
        cst.append(c.load(sb["gnw_c"][i], c.ap["l0a_gnw"][i * 128:(i + 1) * 128, :]))
    for i in range(8):
        cst.append(c.load(sb["l0f_wb_c"][i], c.ap["l0f_wb"][i * 128:(i + 1) * 128, :]))
        cst.append(c.load(sb["bog_c"][i], c.ap["l1a_bog"][i * 128:(i + 1) * 128, :]))
        cst.append(c.load(sb["bout_c"][i], c.ap["l1a_bout"][i * 128:(i + 1) * 128, :]))
        cst.append(c.load(sb["ramp_c"][i], c.ap["l1a_ramp"][i * 128:(i + 1) * 128, :]))
    for gi in range(3):
        for j in range(8):
            cst.append(c.load(sb["cw3"][gi][j],
                              c.ap["l1a_cw"][gi][j * 128:(j + 1) * 128, :]))
    for i in range(32):
        cst.append(c.load(sb["l1f_cw_c"][i], c.ap["l1f_cw"][i * 128:(i + 1) * 128, :]))
        cst.append(c.load(sb["l1f_cb_c"][i], c.ap["l1f_cb"][i * 128:(i + 1) * 128, :]))
    CST = cst[-1]

    if "l0a" in STAGES:
        l0_attn(c, CST)
        p.barrier()
    if "l0f" in STAGES:
        l0_ffn(c)
        p.barrier()
    if "l1a" in STAGES:
        l1_attn(c)
        p.barrier()
    if "l1f" in STAGES:
        l1_ffn(c)
    p.barrier()


def kernel(**inp):
    xs = np.asarray(inp["L0_attn_x"])
    nc = build_program()
    shared = {}
    shared["l0a_wq"] = np.ascontiguousarray(np.asarray(inp["L0_attn_Wq"]).T)
    shared["l0a_wk"] = np.ascontiguousarray(np.asarray(inp["L0_attn_Wk"]).T)
    shared["l0a_wv"] = np.ascontiguousarray(np.asarray(inp["L0_attn_Wv"]).T)
    shared["l0a_wg"] = np.ascontiguousarray(np.asarray(inp["L0_attn_Wg"]).T)
    shared["l0a_wo"] = np.ascontiguousarray(np.asarray(inp["L0_attn_Wo"]).T)
    shared["l0a_gnw"] = np.ascontiguousarray(np.asarray(inp["L0_attn_gnorm_w"])[:, None])
    shared["l0f_wq"] = np.ascontiguousarray(np.asarray(inp["L0_ffn_Wq"]).T)
    rt = np.zeros((1024, 256), np.float32)
    rt[:, 0:255] = np.asarray(inp["L0_ffn_routers"]).T
    shared["l0f_rt"] = rt
    shared["l0f_lv"] = np.ascontiguousarray(np.asarray(inp["L0_ffn_leaf_values"]))
    shared["l0f_wo"] = np.ascontiguousarray(np.asarray(inp["L0_ffn_Wout_w"]).T)
    shared["l0f_wb"] = np.ascontiguousarray(np.asarray(inp["L0_ffn_Wout_b"])[:, None])
    shared["l1a_win"] = np.ascontiguousarray(np.asarray(inp["L1_attn_W_in"]))
    wctrl = np.asarray(inp["L1_attn_W_ctrl"])
    shared["l1a_wab"] = np.ascontiguousarray(wctrl[:, 0:8])
    shared["l1a_wog"] = np.ascontiguousarray(wctrl[:, 8:])
    bctrl = np.asarray(inp["L1_attn_b_ctrl"])
    shared["l1a_bab"] = np.ascontiguousarray(bctrl[0:8][:, None])
    shared["l1a_bog"] = np.ascontiguousarray(bctrl[8:][:, None])
    shared["l1a_cw"] = np.ascontiguousarray(np.stack(
        [np.asarray(inp["L1_attn_w_q"])[:, 0, :],
         np.asarray(inp["L1_attn_w_k"])[:, 0, :],
         np.asarray(inp["L1_attn_w_v"])[:, 0, :]], 0).astype(np.float32))
    shared["l1a_wout"] = np.ascontiguousarray(np.asarray(inp["L1_attn_W_out"]))
    shared["l1a_bout"] = np.ascontiguousarray(np.asarray(inp["L1_attn_b_out"])[:, None])
    shared["l1a_ramp"] = np.ascontiguousarray(np.asarray(inp["L1_attn_ramp_logit"])[:, None])
    shared["l1f_w1"] = np.ascontiguousarray(np.asarray(inp["L1_ffn_w1"]).T)
    shared["l1f_cw"] = np.ascontiguousarray(np.asarray(inp["L1_ffn_conv_w"])[:, 0, :])
    shared["l1f_cb"] = np.ascontiguousarray(np.asarray(inp["L1_ffn_conv_b"])[:, None])
    shared["l1f_w2"] = np.ascontiguousarray(np.asarray(inp["L1_ffn_w2"]).T)
    in_maps = []
    for b in range(4):
        for s in range(2):
            m = dict(shared)
            m["x_in"] = np.ascontiguousarray(
                xs[b, s * TC:(s + 1) * TC, :].T.astype(np.float32))
            m.update(host_tables(s))
            m["is_sec"] = np.full((128, 1), float(s), np.float32)
            m["cvals"] = np.broadcast_to(
                np.array([1e-5, 1e-6, 1.0 / 256.0], np.float32)[None, :],
                (128, 3)).copy()
            m["is_fst"] = np.full((128, 1), float(1 - s), np.float32)
            in_maps.append(m)
    res = run_bass_kernel_spmd(nc, in_maps, core_ids=list(range(8)))
    out = np.zeros((4, 8192, 1024), np.float32)
    for b in range(4):
        for s in range(2):
            out[b, s * TC:(s + 1) * TC, :] = res.results[2 * b + s]["out"].T
    return out
